# version 21
# baseline (speedup 1.0000x reference)
import math

import numpy as np
import ml_dtypes

import concourse.bass as bass
import concourse.mybir as mybir
from concourse.bass_utils import run_bass_kernel_spmd

F32 = mybir.dt.float32
BF16 = mybir.dt.bfloat16
I32 = mybir.dt.int32
AF = mybir.ActivationFunctionType
ALU = mybir.AluOpType
AX = mybir.AxisListType

D = 4096
S = 4096
NC = 8
TPC = 1024
HPC = 8
KVPC = 2
HD = 128
NIT = 32
EXP_OFF = -12.0
INV_SQRT_HD = 1.0 / math.sqrt(HD)

ENGINES = ("sp", "act", "pe", "dve", "pool")



class Stream:
    def __init__(self, name, engine, depth):
        self.name = name
        self.engine = engine
        self.depth = depth
        self.sems = None
        self.uses = [0] * depth
        self.n = 0


class Op:
    __slots__ = ("idx", "engine", "emit", "kind", "reads", "writes", "tag",
                 "deps", "needs_inc", "seq", "event", "waits", "stream",
                 "slot", "val")

    def __init__(self, idx, engine, emit, kind, reads, writes, tag):
        self.idx = idx
        self.engine = engine
        self.emit = emit
        self.kind = kind
        self.reads = tuple(reads)
        self.writes = tuple(writes)
        self.tag = tag
        self.deps = []
        self.needs_inc = False
        self.seq = None
        self.event = None
        self.waits = []
        self.stream = None
        self.slot = None
        self.val = None


def _edge_skippable(p, o):
    return p.kind == "op" and o.kind == "op" and p.engine == "pe" and o.engine == "pe"


class Prog:
    def __init__(self):
        self.ops = []
        self.writers = {}
        self.readers = {}
        self.streams = {}
        self.cc_n = 0

    def stream(self, name, engine, depth=6):
        st = self.streams.get(name)
        if st is None:
            st = self.streams[name] = Stream(name, engine, depth)
        else:
            assert st.engine == engine
        return st

    def _record(self, op):
        akey = (("st", op.stream.name, op.slot) if op.kind == "dma"
                else ("cc",) if op.kind == "cc" else op.engine)
        dep = set()
        for rg in op.reads:
            dep.update(self.writers.get(rg, {}).values())
        for rg in op.writes:
            dep.update(self.writers.get(rg, {}).values())
            dep.update(self.readers.get(rg, {}).values())
        dep.discard(op.idx)
        op.deps = sorted(dep)
        for rg in op.writes:
            self.writers.setdefault(rg, {})[akey] = op.idx
            self.readers[rg] = {}
        for rg in op.reads:
            self.readers.setdefault(rg, {})[akey] = op.idx
        self.ops.append(op)

    def fence(self, keys, tag="fence"):
        self.op("pool", (lambda e: e.memset(self._fence_buf[0:1, 0:1], 0.0)),
                reads=(), writes=tuple(keys), tag=tag)

    def op(self, engine, emit, reads=(), writes=(), tag=""):
        o = Op(len(self.ops), engine, emit, "op", reads, writes, tag)
        self._record(o)
        return o

    def dma(self, stream, emit, reads=(), writes=(), tag=""):
        o = Op(len(self.ops), stream.engine, emit, "dma", reads, writes, tag)
        o.stream = stream
        o.slot = stream.n % stream.depth
        stream.uses[o.slot] += 1
        o.val = stream.uses[o.slot] * 16
        stream.n += 1
        self._record(o)
        return o

    def cc(self, emit, reads=(), writes=(), tag=""):
        o = Op(len(self.ops), "pool", emit, "cc", reads, writes, tag)
        self.cc_n += 1
        o.val = self.cc_n
        self._record(o)
        return o

    def finalize(self, nc, block):
        ops = self.ops
        for o in ops:
            for d in o.deps:
                p = ops[d]
                if p.kind == "op" and not _edge_skippable(p, o):
                    p.needs_inc = True

        eng_sems = {e: nc.semaphore(f"sem_{e}").__enter__() for e in ENGINES}
        cc_sem = nc.semaphore("sem_cc").__enter__() if self.cc_n else None
        for st in self.streams.values():
            st.sems = [nc.semaphore(f"sm_{st.name}{i}").__enter__()
                       for i in range(st.depth)]

        seq = {e: 0 for e in ENGINES}
        for o in ops:
            if o.kind == "op":
                if o.needs_inc:
                    seq[o.engine] += 1
                    o.seq = seq[o.engine]
                    o.event = (("eng", o.engine), o.seq)
            elif o.kind == "dma":
                o.event = (("dma", o.stream.name, o.slot), o.val)
            else:
                o.event = (("cc",), o.val)

        know = {e: {} for e in ENGINES}
        know_at = {}
        for o in ops:
            e = o.engine
            req = []
            for d in o.deps:
                p = ops[d]
                if p.event is None:
                    assert _edge_skippable(p, o), f"no event: {p.tag} <- {o.tag}"
                    continue
                if _edge_skippable(p, o):
                    continue
                req.append((p.event, d))
            if o.kind == "dma" and o.val > 16:
                req.append(((("dma", o.stream.name, o.slot), o.val - 16), None))
            for (sk, v), d in req:
                if know[e].get(sk, 0) >= v:
                    continue
                o.waits.append((sk, v))
                know[e][sk] = v
                if d is not None:
                    for k2, v2 in know_at[d].items():
                        if know[e].get(k2, 0) < v2:
                            know[e][k2] = v2
            snap = dict(know[e])
            if o.event is not None:
                sk, v = o.event
                if snap.get(sk, 0) < v:
                    snap[sk] = v
            know_at[o.idx] = snap

        def sem_of(sk):
            if sk[0] == "eng":
                return eng_sems[sk[1]]
            if sk[0] == "dma":
                return self.streams[sk[1]].sems[sk[2]]
            return cc_sem

        drain = []
        for st in self.streams.values():
            for slot in range(st.depth):
                if st.uses[slot]:
                    drain.append((("dma", st.name, slot), st.uses[slot] * 16))
        if cc_sem is not None:
            drain.append((("cc",), self.cc_n))
        drain = [(sk, v) for sk, v in drain if know["sp"].get(sk, 0) < v]

        eng_map = {"sp": block.sync, "act": block.scalar, "pe": block.tensor,
                   "dve": block.vector, "pool": block.gpsimd}
        by_eng = {e: [] for e in ENGINES}
        for o in ops:
            by_eng[o.engine].append(o)

        def emit_for(eng_name):
            def run(eng):
                for o in by_eng[eng_name]:
                    for sk, v in o.waits:
                        eng.wait_ge(sem_of(sk), v)
                    inst = o.emit(eng)
                    if o.kind == "dma":
                        inst.then_inc(o.stream.sems[o.slot], 16)
                    elif o.kind == "cc":
                        inst.then_inc(cc_sem)
                    elif o.needs_inc:
                        inst.then_inc(eng_sems[o.engine], 1)
                if eng_name == "sp":
                    for sk, v in drain:
                        eng.wait_ge(sem_of(sk), v)
            return run

        for name in ENGINES:
            if by_eng[name] or name == "sp":
                eng_map[name](emit_for(name))



def make_consts():
    bf = ml_dtypes.bfloat16
    inv_freq = 1.0 / (10000.0 ** (np.arange(64, dtype=np.float64) / 64.0))
    ang = np.arange(S, dtype=np.float64)[None, :] * inv_freq[:, None]
    cosd = np.concatenate([np.cos(ang), np.cos(ang)], axis=0)
    sinds = np.concatenate([-np.sin(ang), np.sin(ang)], axis=0)
    cosd = cosd.astype(np.float32)
    sinds = sinds.astype(np.float32)
    kk = np.arange(128)[:, None]
    qq = np.arange(1024)[None, :]
    masks = np.concatenate(
        [((u * 128 + kk) <= qq)[:, (0 if u < 4 else 512):
                                (512 if u < 4 else 1024)].astype(np.float32)
         for u in range(8)],
        axis=1).astype(bf)
    selblk = np.zeros((128, 4096), dtype=np.float32)
    p_ = np.arange(128)
    for t in range(32):
        selblk[4 * t + p_ // 32, t * 128 + p_] = 1.0
    selblk = selblk.astype(bf)
    sel32h = np.zeros((32, 4096), dtype=np.float32)
    for g in range(32):
        sel32h[g, g * 128:(g + 1) * 128] = 0.5
    sel32h = sel32h.astype(bf)
    identf = np.eye(128, dtype=np.float32)
    onesb = np.ones((1, 128), dtype=bf)
    onesc = np.ones((128, 1), dtype=bf)
    cvec = np.zeros((128, 8), dtype=np.float32)
    cvec[:, 0] = INV_SQRT_HD
    cvec[:, 1] = EXP_OFF
    cvec[:, 2] = 1.0 / D
    cvec[:, 3] = 1e-5
    cvec[:, 4] = 1.0 / math.sqrt(32.0)
    cvec[:, 6] = -1.0
    return dict(c_cosd=cosd, c_sinds=sinds, c_masks=masks, c_selblk=selblk,
                c_sel32h=sel32h, c_identf=identf, c_onesb=onesb,
                c_onesc=onesc, c_cvec=cvec)


CONST_DECLS = [
    ("c_cosd", [128, S], F32), ("c_sinds", [128, S], F32),
    ("c_masks", [128, 4096], BF16),
    ("c_selblk", [128, 4096], BF16), ("c_sel32h", [32, 4096], BF16),
    ("c_identf", [128, 128], F32), ("c_onesb", [1, 128], BF16),
    ("c_onesc", [128, 1], BF16), ("c_cvec", [128, 8], F32),
]



class Ctx:
    pass


def new_ctx():
    cx = Ctx()
    cx.nc = bass.Bass("TRN2", target_bir_lowering=False)
    cx.pg = Prog()
    cx.pg._fence_buf = cx.nc.alloc_sbuf_tensor("fence_buf", [1, 4], F32)
    cx.sb = {}
    cx.ps = {}
    cx.dr = {}
    return cx


def sb_alloc(cx, name, shape, dtype):
    h = cx.nc.alloc_sbuf_tensor(name, shape, dtype)
    cx.sb[name] = h
    return h


def dram(cx, name, shape, dtype, kind="Internal"):
    h = cx.nc.dram_tensor(name, shape, dtype, kind=kind)
    cx.dr[name] = h
    return h


class HalfBank:

    def __init__(self, t, off):
        self.t = t
        self.off = off

    def __getitem__(self, idx):
        if not isinstance(idx, tuple):
            rows, cols = idx, slice(None)
        else:
            rows, cols = idx
        if cols == slice(None):
            cols = slice(0, 512)
        start = (cols.start or 0) + self.off
        stop = (512 if cols.stop is None else cols.stop) + self.off
        return self.t[rows, start:stop]


def decl_psum(cx):
    cx.pp = [cx.nc.alloc_psum_tensor(f"pp{i}", [128, 1024], F32)
             for i in range(4)]
    for i in range(8):
        cx.ps[i] = HalfBank(cx.pp[i // 2], (i % 2) * 512)


def decl_common(cx, const_names=None):
    ld = cx.pg.stream("cload", "sp", depth=4)
    ldc = cx.pg.stream("cloadc", "pool", depth=2)
    for name, shape, dt in CONST_DECLS:
        if const_names is not None and name not in const_names:
            continue
        dram(cx, name, shape, dt, kind="ExternalInput")
        if name in ("c_masks", "c_selblk"):
            continue
        if name in ("c_cosd", "c_sinds"):
            s = sb_alloc(cx, name[2:], shape, BF16)
            d = cx.dr[name]
            cx.pg.dma(ldc, (lambda e, a=s, b=d: e.dma_start(out=a[:], in_=b.ap())),
                      reads=[name], writes=[name[2:]], tag=f"ld_{name}")
            continue
        s = sb_alloc(cx, name[2:], shape, dt)
        d = cx.dr[name]
        cx.pg.dma(ld, (lambda e, a=s, b=d: e.dma_start(out=a[:], in_=b.ap())),
                  reads=[name], writes=[name[2:]], tag=f"ld_{name}")
    decl_psum(cx)


def cv(cx, j, p=128):
    return cx.sb["cvec"][0:p, j:j + 1]



def emit_ln_stats_reset(cx, tag):
    return {
        "pxs": [(tag, "pxs", tb) for tb in range(2)],
        "pss": [(tag, "pss", tb) for tb in range(2)],
    }


def emit_ln_tile_stats(cx, tag, it, rxb, xbv, rsq_src, sq_t, first, last):
    pg = cx.pg
    onesc = cx.sb["onesc"]
    for tb in range(2):
        pg.op("pe", (lambda e, b=xbv, t=tb, s=first, z=last:
                     e.matmul(cx.pp[2][0:1, t * 512:(t + 1) * 512],
                              onesc[:, 0:1], b[:, t * 512:(t + 1) * 512],
                              start=s, stop=z)),
              reads=[rxb, "onesc"], writes=[("PB", 4 + tb)],
              tag=f"{tag}mmx{it}_{tb}")
        pg.op("pe", (lambda e, b=sq_t, t=tb, s=first, z=last:
                     e.matmul(cx.pp[3][0:1, t * 512:(t + 1) * 512],
                              onesc[:, 0:1], b[:, t * 512:(t + 1) * 512],
                              start=s, stop=z)),
              reads=[rsq_src, "onesc"], writes=[("PB", 6 + tb)],
              tag=f"{tag}mms{it}_{tb}")


def emit_ln_finalize(cx, tag):
    pg = cx.pg
    stats = cx.sb["stats"]
    srow = cx.sb["srow"]
    bct = cx.sb["bct"]
    onesb = cx.sb["onesb"]
    pxs = [cx.ps[4], cx.ps[5]]
    pss = [cx.ps[6], cx.ps[7]]
    pbc = cx.ps[3]
    rstats = ("LN", "stats")
    rsrow = ("LN", "srow")
    rbct = ("LN", "bct")
    for tb in range(2):
        mv = stats[0:1, tb * 512:(tb + 1) * 512]
        vv = stats[0:1, 1024 + tb * 512:1024 + (tb + 1) * 512]
        sc = stats[0:1, 2048 + tb * 512:2048 + (tb + 1) * 512]
        pg.op("dve", (lambda e, a=mv, p=pxs[tb]:
                      e.tensor_scalar(a, p[0:1, :], 1.0 / D, None, ALU.mult)),
              reads=[("PB", 4 + tb)], writes=[rstats], tag=f"{tag}m{tb}")
        pg.op("dve", (lambda e, a=vv, p=pss[tb]:
                      e.tensor_scalar(a, p[0:1, :], 1.0 / D, None, ALU.mult)),
              reads=[("PB", 6 + tb)], writes=[rstats], tag=f"{tag}v{tb}")
        pg.op("dve", (lambda e, a=sc, m=mv: e.tensor_mul(a, m, m)),
              reads=[rstats], writes=[rstats], tag=f"{tag}m2{tb}")
        pg.op("dve", (lambda e, a=vv, b=sc: e.tensor_sub(a, a, b)),
              reads=[rstats], writes=[rstats], tag=f"{tag}var{tb}")
        pg.op("act", (lambda e, a=vv: e.activation(a, a, AF.Sqrt, bias=cv(cx, 3, 1))),
              reads=[rstats, "cvec"], writes=[rstats], tag=f"{tag}sd{tb}")
        pg.op("dve", (lambda e, a=vv: e.reciprocal(a, a)),
              reads=[rstats], writes=[rstats], tag=f"{tag}rs{tb}")
        pg.op("dve", (lambda e, a=sc, m=mv, s=vv: e.tensor_mul(a, m, s)),
              reads=[rstats], writes=[rstats], tag=f"{tag}ms{tb}")
        pg.op("dve", (lambda e, a=sc: e.tensor_scalar(a, a, -1.0, None, ALU.mult)),
              reads=[rstats], writes=[rstats], tag=f"{tag}nms{tb}")
    pg.op("dve", (lambda e: e.tensor_copy(srow[0:1, 0:1024], stats[0:1, 1024:2048])),
          reads=[rstats], writes=[rsrow], tag=f"{tag}cr1")
    pg.op("dve", (lambda e: e.tensor_copy(srow[0:1, 1024:2048], stats[0:1, 2048:3072])),
          reads=[rstats], writes=[rsrow], tag=f"{tag}cr2")
    for j in range(4):
        src = srow[0:1, j * 512:(j + 1) * 512]
        dst = bct[:, j * 512:(j + 1) * 512]
        pg.op("pe", (lambda e, s_=src: e.matmul(pbc[:, 0:512], onesb[0:1, :], s_,
                                                start=True, stop=True)),
              reads=[rsrow, "onesb"], writes=[("PB", 3)], tag=f"{tag}bc{j}")
        pg.op("dve", (lambda e, d_=dst: e.tensor_copy(d_, pbc[:, 0:512])),
              reads=[("PB", 3)], writes=[rbct], tag=f"{tag}bcc{j}")
    return rbct


def emit_ln_apply(cx, tag, it, xbv, rxb, rbct, xn_dst_name, xn_dst, st_out):
    pg = cx.pg
    bct = cx.sb["bct"]
    pg.op("dve", (lambda e, a=xbv: e.tensor_mul(a, a, bct[:, 0:1024])),
          reads=[rxb, rbct], writes=[rxb], tag=f"{tag}ap1_{it}")
    pg.op("dve", (lambda e, a=xbv: e.tensor_add(a, a, bct[:, 1024:2048])),
          reads=[rxb, rbct], writes=[rxb], tag=f"{tag}ap2_{it}")
    if xn_dst is not None:
        pg.dma(st_out, (lambda e, a=xbv, i=it:
                        e.dma_start(out=xn_dst.ap()[i * 128:(i + 1) * 128, :], in_=a)),
               reads=[rxb], writes=[("XD", xn_dst_name, it, 0),
                                    ("XD", xn_dst_name, it, 1)],
               tag=f"{tag}sxn{it}")


def emit_ln_standalone(cx, x_src_name, xn_dst_name, tag):
    pg = cx.pg
    x_src = cx.dr[x_src_name]
    xn_dst = cx.dr[xn_dst_name] if xn_dst_name else None
    ld = pg.stream("xload", "sp", depth=6)
    st_out = pg.stream("agout", "act", depth=6)
    xb = cx.sb["xb_slab"]
    xts = [cx.sb["xtile0"], cx.sb["xtile1"]]
    sqs = [cx.sb["sq0"], cx.sb["sq1"]]
    for it in range(NIT):
        xt = xts[it % 2]
        sq_t = sqs[it % 2]
        rx = ("LN", "xt", it % 2)
        rsq = ("LN", "sq", it % 2)
        rxb = (tag, "xb", it)
        pg.dma(ld, (lambda e, a=xt, b=x_src, i=it:
                    e.dma_start(out=a[:], in_=b.ap()[i * 128:(i + 1) * 128, :])),
               reads=[x_src_name], writes=[rx], tag=f"{tag}ldx{it}")
        xbv = xb[:, it * 1024:(it + 1) * 1024]
        pg.op("dve", (lambda e, a=xbv, b=xt: e.tensor_copy(a, b[:])),
              reads=[rx], writes=[rxb], tag=f"{tag}cast{it}")
        pg.op("act", (lambda e, a=sq_t, b=xt: e.activation(a[:], b[:], AF.Square)),
              reads=[rx], writes=[rsq], tag=f"{tag}sq{it}")
        emit_ln_tile_stats(cx, tag, it, rxb, xbv, rsq, sq_t[:],
                           first=(it == 0), last=(it == NIT - 1))
    rbct = emit_ln_finalize(cx, tag)
    for it in range(NIT):
        xbv = xb[:, it * 1024:(it + 1) * 1024]
        emit_ln_apply(cx, tag, it, xbv, (tag, "xb", it), rbct,
                      xn_dst_name, xn_dst, st_out)



def build_segment_A():
    cx = new_ctx()
    nc = cx.nc
    decl_common(cx, const_names={"c_onesb", "c_onesc", "c_cvec"})
    dram(cx, "x_fm", [D, TPC], F32, kind="ExternalInput")
    dram(cx, "xn0", [D, TPC], BF16, kind="ExternalOutput")
    sb_alloc(cx, "xb_slab", [128, NIT * 1024], BF16)
    sb_alloc(cx, "xtile0", [128, 1024], F32)
    sb_alloc(cx, "xtile1", [128, 1024], F32)
    sb_alloc(cx, "sq0", [128, 1024], BF16)
    sb_alloc(cx, "sq1", [128, 1024], BF16)
    sb_alloc(cx, "stats", [1, 3072], F32)
    sb_alloc(cx, "srow", [1, 2048], BF16)
    sb_alloc(cx, "bct", [128, 2048], BF16)
    emit_ln_standalone(cx, "x_fm", "xn0", "lnA")
    with nc.Block() as block:
        cx.pg.finalize(nc, block)
    nc.finalize()
    return cx





def emit_dequant_qkv_to_dram(cx, tag):
    pg = cx.pg
    sc = cx.sb["scalesT"]
    sel = cx.sb["sel32h"]
    ldsc = pg.stream("scload", "pool", depth=2)
    ldcd = pg.stream("cdload", "sp", depth=4)
    ldcd2 = pg.stream("cdload2", "pool", depth=4)
    stw = pg.stream("wdq", "act", depth=6)
    for nm, off, w in [("l1_q_sc", 0, 1024), ("l1_k_sc", 1024, 256),
                       ("l1_v_sc", 1280, 256)]:
        d = cx.dr[nm]
        pg.dma(ldsc, (lambda e, d_=d, o=off, w_=w:
                      e.dma_start(out=sc[0:32, o:o + w_], in_=d_.ap())),
               reads=[nm], writes=[(tag, "sc")], tag=f"{tag}ld{nm}")
    cdb = cx.sb["cd_buf0"]
    wdq = [cx.sb[f"wdq{j}"] for j in range(3)]
    ps_scs = [cx.ps[0], cx.ps[1]]
    blocks = [("l1_q_cd", 0, 512, 0, "wq_deq", 0),
              ("l1_q_cd", 512, 512, 512, "wq_deq", 512),
              ("l1_k_cd", 0, 256, 1024, "wk_deq", 0),
              ("l1_v_cd", 0, 256, 1280, "wv_deq", 0)]
    nb_tot = 0
    for it in range(NIT):
        for nm, scol, w, sccol, dnm, dcol in blocks:
            half = nb_tot % 2
            cbv = cdb[:, half * 512:half * 512 + w]
            rcb = (tag, "cd", half)
            d = cx.dr[nm]
            pg.dma(ldcd if half == 0 else ldcd2,
                   (lambda e, d_=d, a=cbv, i=it, sc0=scol, w_=w:
                    e.dma_start(out=a,
                                in_=d_.ap()[i * 128:(i + 1) * 128,
                                            sc0:sc0 + w_])),
                   reads=[nm], writes=[rcb], tag=f"{tag}cd{it}_{nm}{scol}")
            wb = wdq[nb_tot % 3]
            rwb = (tag, "wdq", nb_tot % 3)
            nb_tot += 1
            ps_sc = ps_scs[nb_tot % 2]
            pbk = ("PB", 0 + nb_tot % 2)
            pg.op("pe", (lambda e, i=it, o=sccol, w_=w, p=ps_sc:
                         e.matmul(p[:, 0:w_], sel[:, i * 128:(i + 1) * 128],
                                  sc[0:32, o:o + w_], start=True, stop=True)),
                  reads=[(tag, "sc"), "sel32h"], writes=[pbk],
                  tag=f"{tag}scmm{it}_{sccol}")
            pg.op("dve", (lambda e, b=cbv, w_=w, a=wb, p=ps_sc:
                          e.scalar_tensor_tensor(a[:, 0:w_], b, -8.0,
                                                 p[:, 0:w_], ALU.add, ALU.mult)),
                  reads=[rcb, pbk], writes=[rwb], tag=f"{tag}dq{it}_{sccol}")
            dd = cx.dr[dnm]
            pg.dma(stw, (lambda e, a=wb, d_=dd, i=it, c=dcol, w_=w:
                         e.dma_start(out=d_.ap()[i * 128:(i + 1) * 128, c:c + w_],
                                     in_=a[:, 0:w_])),
                   reads=[rwb], writes=[dnm], tag=f"{tag}wst{it}_{sccol}")


QKV_UNITS = ([("q", i) for i in range(8)] + [("k", 0), ("k", 1),
             ("v", 0), ("v", 1)])


def emit_qkv(cx, l, tag):
    pg = cx.pg
    xb = cx.sb["xb_slab"]
    wu = [cx.sb[f"wu{j}"] for j in range(3)]
    cosd = cx.sb["cosd"]
    sinds = cx.sb["sinds"]
    qsets = [(cx.sb["qstage0"], cx.sb["qswap0"], cx.sb["qrot0"], cx.sb["utmp0"]),
             (cx.sb["qstage1"], cx.sb["qswap1"], cx.sb["qrot1"], cx.sb["utmp1"])]
    vst = cx.sb["vstage"]
    vbf = cx.sb["vbf"]
    identf = cx.sb["identf"]
    ldxn = pg.stream("xnload", "sp", depth=3)
    swp = pg.stream("swp", "act", depth=4)
    ldw = pg.stream(f"wload{l}", "pool" if l == 0 else "sp", depth=3)
    stq = pg.stream("qout", "act", depth=6)
    xn_all = cx.dr["xn_all"]
    q_scr = cx.dr["q_scr"]
    k_scr = cx.dr["k_scr"]
    v_scr = cx.dr["v_scr"]
    if l == 0:
        wsrc = {"q": cx.dr["l0_wq"], "k": cx.dr["l0_wk"], "v": cx.dr["l0_wv"]}
    else:
        wsrc = {"q": cx.dr["wq_deq"], "k": cx.dr["wk_deq"], "v": cx.dr["wv_deq"]}
    banks = [cx.ps[j] for j in range(8)]
    bank_ctr = 0
    for rb in range(4):
        rxn = (tag, "xn")
        for f in range(4):
            pg.dma(ldxn, (lambda e, r=rb, f_=f: e.dma_start(
                out=xb[:, f_ * 8192:(f_ + 1) * 8192].rearrange(
                    "p (a c) -> p a c", c=1024),
                in_=xn_all.ap()[f_ * D + r * 1024:f_ * D + (r + 1) * 1024,
                                :].rearrange("(a p) c -> p a c", p=128))),
                reads=[("xn_all", f)], writes=[rxn], tag=f"{tag}xn{rb}_{f}")
        for ui, (kind, idx) in enumerate(QKV_UNITS):
            wb = wu[(rb * 12 + ui) % 3]
            rw = (tag, "wu", (rb * 12 + ui) % 3)
            src_d = wsrc[kind]
            col0 = idx * 128
            pg.dma(ldw, (lambda e, s=src_d, c0=col0, w_=wb:
                         e.dma_start(out=w_[:].rearrange("p (a c) -> p a c", c=128),
                                     in_=s.ap()[:, c0:c0 + 128].rearrange(
                                         "(a p) c -> p a c", p=128))),
                   reads=[src_d.name], writes=[rw], tag=f"{tag}w{rb}_{ui}")
            b0 = bank_ctr % 8
            bank_ctr += 2
            pbs = [banks[b0], banks[(b0 + 1) % 8]]
            pair = cx.pp[b0 // 2]
            for it in range(NIT):
                first, last = it == 0, it == NIT - 1
                for tb in range(2):
                    pg.op("pe", (lambda e, p=pair, w_=wb, i=it, t_=tb,
                                 s=first, z=last:
                                 e.matmul(p[:, t_ * 512:(t_ + 1) * 512],
                                          w_[:, i * 128:(i + 1) * 128],
                                          xb[:, i * 1024 + t_ * 512:
                                             i * 1024 + (t_ + 1) * 512],
                                          start=s, stop=z)),
                          reads=[rw, rxn], writes=[("PB", b0 + tb)],
                          tag=f"{tag}mm{rb}_{ui}_{it}{tb}")
            for tb in range(2):
                pb = pbs[tb]
                rpb = ("PB", (b0 + tb) % 8)
                tcol = rb * 1024 + tb * 512
                if kind in ("q", "k"):
                    si = (rb * 24 + ui * 2 + tb) % 2
                    qst, qsw, qrot, ut = qsets[si]
                    rqs = (tag, "qst", si)
                    rqw = (tag, "qsw", si)
                    rqr = (tag, "qrot", si)
                    rut = (tag, "ut", si)
                    pg.op("act", (lambda e, p=pb, a=qst: e.copy(a[:], p[:])),
                          reads=[rpb], writes=[rqs], tag=f"{tag}ev{rb}_{ui}{tb}")
                    pg.dma(swp, (lambda e, a=qst, b=qsw: e.dma_start(
                        out=b[0:64, :], in_=a[64:128, :])),
                        reads=[rqs], writes=[rqw], tag=f"{tag}sw1_{rb}{ui}{tb}")
                    pg.dma(swp, (lambda e, a=qst, b=qsw: e.dma_start(
                        out=b[64:128, :], in_=a[0:64, :])),
                        reads=[rqs], writes=[rqw], tag=f"{tag}sw2_{rb}{ui}{tb}")
                    c_sl = cosd[:, tcol:tcol + 512]
                    s_sl = sinds[:, tcol:tcol + 512]
                    pg.op("dve", (lambda e, a=qst, u=ut, c=c_sl:
                                  e.tensor_mul(u[:, 0:512], a[:], c)),
                          reads=[rqs, "cosd"], writes=[rut], tag=f"{tag}r1_{rb}{ui}{tb}")
                    pg.op("dve", (lambda e, b=qsw, u=ut, s=s_sl:
                                  e.tensor_mul(u[:, 512:1024], b[:], s)),
                          reads=[rqw, "sinds"], writes=[rut], tag=f"{tag}r2_{rb}{ui}{tb}")
                    pg.op("dve", (lambda e, u=ut, o=qrot:
                                  e.tensor_add(o[:], u[:, 0:512], u[:, 512:1024])),
                          reads=[rut], writes=[rqr], tag=f"{tag}r3_{rb}{ui}{tb}")
                    dst = q_scr if kind == "q" else k_scr
                    dkey = ("q_scr", idx) if kind == "q" else ("k_scr", idx)
                    row0 = idx * 128
                    pg.dma(stq, (lambda e, o=qrot, d_=dst, r0=row0, c0=tcol:
                                 e.dma_start(out=d_.ap()[r0:r0 + 128, c0:c0 + 512],
                                             in_=o[:])),
                           reads=[rqr], writes=[dkey], tag=f"{tag}st{rb}_{ui}{tb}")
                else:
                    rvs = (tag, "vst")
                    rvb = (tag, "vbf")
                    pg.op("dve", (lambda e, p=pb: e.tensor_copy(vst[:], p[:])),
                          reads=[rpb], writes=[rvs], tag=f"{tag}vev{rb}{tb}")
                    for sub in range(4):
                        pg.op("pe", (lambda e, p=pb, s_=sub:
                                     e.transpose(p[:, 0:128],
                                                 vst[:, s_ * 128:(s_ + 1) * 128],
                                                 identf[:])),
                              reads=[rvs, "identf"], writes=[rpb],
                              tag=f"{tag}vtr{rb}{tb}{sub}")
                        pg.op("dve", (lambda e, p=pb: e.tensor_copy(vbf[:, 0:128],
                                                                    p[:, 0:128])),
                              reads=[rpb], writes=[rvb], tag=f"{tag}vcp{rb}{tb}{sub}")
                        trow = tcol + sub * 128
                        pg.dma(stq, (lambda e, r0=trow, c0=idx * 128:
                                     e.dma_start(
                                         out=v_scr.ap()[r0:r0 + 128, c0:c0 + 128],
                                         in_=vbf[:, 0:128])),
                               reads=[rvb], writes=[("v_scr", idx)],
                               tag=f"{tag}vst{rb}{tb}{sub}")


def emit_attn(cx, tag, dst_name="attn_my", chunk_cc=None, work_cb=None):
    pg = cx.pg
    q_scr, k_scr, v_scr = cx.dr["q_scr"], cx.dr["k_scr"], cx.dr["v_scr"]
    attn_my = cx.dr[dst_name]
    mld = pg.stream("mld", "pool", depth=1)
    mk_d = cx.dr["c_masks"]
    pg.dma(mld, (lambda e: e.dma_start(out=cx.sb["masks"][:], in_=mk_d.ap())),
           reads=["c_masks"], writes=["masks"], tag=f"{tag}mld")
    qb_bufs = [cx.sb["q_buf0"], cx.sb["q_buf1"]]
    k_bufs = [cx.sb["k_buf0"], cx.sb["k_buf1"]]
    vb = cx.sb["v_buf0"]
    p_bufs = [cx.sb[f"p_buf{j}"] for j in range(4)]
    lnl = cx.sb["lnl2"]
    rinv = cx.sb["rinv_sb"]
    lsel = cx.sb["lsel"]
    onesbf = cx.sb["onesbf"]
    ast = [cx.sb["a_st0"], cx.sb["a_st1"]]
    masks = cx.sb["masks"]
    onessq = cx.sb["onessq"]
    ldq = pg.stream("qld", "sp", depth=2)
    ldkv = pg.stream("kvld", "sp", depth=2)
    sta = pg.stream("aout", "act", depth=4)
    pp = cx.pp
    rv = (tag, "vbuf")
    rr = (tag, "rinv")
    for qg in range(4):
        nkt = 8 * (qg + 1)
        for h in range(HPC):
            qb_ = qb_bufs[h % 2]
            rq = (tag, "q", h % 2)
            pg.dma(ldq, (lambda e, b=qb_, h_=h, g=qg: e.dma_start(
                out=b[:], in_=q_scr.ap()[h_ * 128:(h_ + 1) * 128,
                                         g * 1024:(g + 1) * 1024])),
                reads=[("q_scr", h)], writes=[rq], tag=f"{tag}ldq{qg}_{h}")
            kvl = h // 4
            kb = k_bufs[kvl]
            rk = (tag, "kbuf", kvl)
            if h % 4 == 0:
                pg.dma(ldkv, (lambda e, b=kb, kv=kvl, n=nkt: e.dma_start(
                    out=b[:, 0:n * 128],
                    in_=k_scr.ap()[kv * 128:(kv + 1) * 128, 0:n * 128])),
                    reads=[("k_scr", kvl)], writes=[rk], tag=f"{tag}ldk{qg}_{kvl}")
                pg.dma(ldkv, (lambda e, kv=kvl, n=nkt: e.dma_start(
                    out=vb[:, 0:n * 128].rearrange("p (a c) -> p a c", c=128),
                    in_=v_scr.ap()[0:n * 128, kv * 128:(kv + 1) * 128].rearrange(
                        "(a p) c -> p a c", p=128))),
                    reads=[("v_scr", kvl)], writes=[rv], tag=f"{tag}ldv{qg}_{kvl}")
            nkt0 = 8 * qg + 4

            def emit_pv(kt, nkt=nkt, nkt0=nkt0, rv=rv):
                pb = p_bufs[kt % 4]
                rp = (tag, "p", kt % 4)
                onesc = cx.sb["onesc"]
                for hf in range(2):
                    if hf == 0 and kt >= nkt0:
                        continue
                    z = (kt == nkt0 - 1) if hf == 0 else (kt == nkt - 1)
                    pg.op("pe", (lambda e, o=pb, c0=kt * 128, s=(kt == 0),
                                 z=z, q0=hf * 512:
                                 e.matmul(pp[2][:, q0:q0 + 512],
                                          vb[:, c0:c0 + 128],
                                          o[:, q0:q0 + 512], start=s, stop=z)),
                          reads=[rv, rp], writes=[("PB", 4 + hf)],
                          tag=f"{tag}pv{qg}_{h}_{kt}_{hf}")
                if kt < nkt0:
                    pg.op("pe", (lambda e, o=pb, s=(kt == 0),
                                 z=(kt == nkt0 - 1):
                                 e.matmul(cx.ps[6][0:1, 0:512], onesc[:, 0:1],
                                          o[:, 0:512], start=s, stop=z)),
                          reads=[rp, "onesc"], writes=[("PB", 6)],
                          tag=f"{tag}la{qg}_{h}_{kt}")
                pg.op("pe", (lambda e, o=pb, s=(kt == 0), z=(kt == nkt - 1):
                             e.matmul(cx.ps[6][32:33, 0:512], onesc[:, 0:1],
                                      o[:, 512:1024], start=s, stop=z)),
                      reads=[rp, "onesc"], writes=[("PB", 6)],
                      tag=f"{tag}lc{qg}_{h}_{kt}")

            for kt in range(nkt):
                live0 = kt < nkt0
                sp_ = pp[kt % 2]
                rs = [("PB", (kt % 2) * 2), ("PB", (kt % 2) * 2 + 1)]
                for hf in range(2):
                    if hf == 0 and not live0:
                        continue
                    pg.op("pe", (lambda e, p=sp_, c0=kt * 128, q_=qb_,
                                 k_=kb, o=hf * 512:
                                 e.matmul(p[:, o:o + 512], k_[:, c0:c0 + 128],
                                          q_[:, o:o + 512],
                                          start=True, stop=True)),
                          reads=[rk, rq], writes=[rs[hf]],
                          tag=f"{tag}s{qg}_{h}_{kt}_{hf}")
                pb = p_bufs[kt % 4]
                rp = (tag, "p", kt % 4)
                if live0:
                    pg.op("act", (lambda e, p=sp_, o=pb:
                                  e.activation(o[:], p[:], AF.Exp, bias=cv(cx, 1),
                                               scale=cv(cx, 0))),
                          reads=rs + ["cvec"], writes=[rp],
                          tag=f"{tag}e{qg}_{h}_{kt}")
                else:
                    pg.op("act", (lambda e, p=sp_, o=pb:
                                  e.activation(o[:, 512:1024], p[:, 512:1024],
                                               AF.Exp, bias=cv(cx, 1),
                                               scale=cv(cx, 0))),
                          reads=[rs[1], "cvec"], writes=[rp],
                          tag=f"{tag}e{qg}_{h}_{kt}")
                if kt >= 8 * qg:
                    j = kt - 8 * qg
                    c0m = 0 if j < 4 else 512
                    pg.op("dve", (lambda e, o=pb, j_=j, c_=c0m:
                                  e.tensor_mul(o[:, c_:c_ + 512],
                                               o[:, c_:c_ + 512],
                                               masks[:, j_ * 512:(j_ + 1) * 512])),
                          reads=[rp, "masks"], writes=[rp],
                          tag=f"{tag}m{qg}_{h}_{kt}")
                if kt >= 1:
                    emit_pv(kt - 1)
                if work_cb is not None:
                    work_cb()
            emit_pv(nkt - 1)
            pg.op("act", (lambda e: e.activation(lnl[:], cx.ps[6][0:33, 0:512],
                                                 AF.Ln)),
                  reads=[("PB", 6)], writes=[(tag, "lnl")],
                  tag=f"{tag}rl{qg}_{h}")
            for hf in range(2):
                r0l = hf * 32
                pg.op("pe", (lambda e, o=hf * 512, f=r0l:
                             e.matmul(pp[1][:, o:o + 512], onesbf[f:f + 1, :],
                                      lnl[f:f + 1, 0:512],
                                      start=True, stop=True)),
                      reads=[(tag, "lnl"), "onesbf"], writes=[("PB", 2 + hf)],
                      tag=f"{tag}bc{qg}_{h}_{hf}")
            pg.op("act", (lambda e: e.activation(rinv[:], pp[1][:], AF.Exp,
                                                 scale=cv(cx, 6))),
                  reads=[("PB", 2), ("PB", 3), "cvec"], writes=[rr],
                  tag=f"{tag}rc{qg}_{h}")
            av = ast[h % 2]
            ra = (tag, "ast", h % 2)
            pg.op("dve", (lambda e, a=av: e.tensor_mul(a[:], pp[2][:], rinv[:])),
                  reads=[("PB", 4), ("PB", 5), rr], writes=[ra],
                  tag=f"{tag}nm{qg}_{h}")
            r0 = qg * 1024 + h * 128
            pg.dma(sta, (lambda e, a=av, r_=r0:
                         e.dma_start(out=attn_my.ap()[r_:r_ + 128, :], in_=a[:])),
                   reads=[ra], writes=[("a2a", qg)], tag=f"{tag}ao{qg}_{h}")
        if chunk_cc is not None:
            chunk_cc(qg)


ATTN_BUF_SPECS = [
    ("q_buf0", [128, 1024], BF16, 2048), ("q_buf1", [128, 1024], BF16, 2048),
    ("k_buf0", [128, 4096], BF16, 8192), ("k_buf1", [128, 4096], BF16, 8192),
    ("v_buf0", [128, 4096], BF16, 8192),
    ("p_buf0", [128, 1024], BF16, 2048), ("p_buf1", [128, 1024], BF16, 2048),
    ("p_buf2", [128, 1024], BF16, 2048), ("p_buf3", [128, 1024], BF16, 2048),
    ("lnl2", [33, 512], F32, 2048),
    ("rinv_sb", [128, 1024], F32, 4096),
    ("a_st0", [128, 1024], BF16, 2048), ("a_st1", [128, 1024], BF16, 2048),
]
DQ_BUF_SPECS = [
    ("scalesT", [32, 1536], BF16, 3072),
    ("cd_buf0", [128, 1024], I32, 4096),
    ("wdq0", [128, 512], BF16, 1024), ("wdq1", [128, 512], BF16, 1024),
    ("wdq2", [128, 512], BF16, 1024),
]


def decl_segB(cx, l):
    sb_alloc(cx, "masks", [128, 4096], BF16)
    sb_alloc(cx, "onessq", [128, 128], F32)
    sb_alloc(cx, "lsel", [128, 4], BF16)
    sb_alloc(cx, "onesbf", [33, 128], F32)
    sb_alloc(cx, "xb_slab", [128, NIT * 1024], BF16)
    sb_alloc(cx, "wgrp0", [128, NIT * 512], BF16)
    sb_alloc(cx, "wgrp1", [128, NIT * 512], BF16)
    sb_alloc(cx, "qstage", [128, 512], F32)
    sb_alloc(cx, "qswap", [128, 512], F32)
    sb_alloc(cx, "qrot", [128, 512], BF16)
    sb_alloc(cx, "utmp", [128, 1024], F32)
    sb_alloc(cx, "vstage", [128, 512], F32)
    sb_alloc(cx, "vbf", [128, 512], BF16)
    sb_alloc(cx, "onescf", [128, 1], F32)
    sb_alloc(cx, "onesrf", [1, 128], F32)
    base = cx.nc.lookup_mloc(cx.sb["wgrp0"]).addr
    off = base
    for nm, shape, dt, sz in ATTN_BUF_SPECS:
        cx.sb[nm] = cx.nc.alloc_sbuf_tensor_at(nm, shape, dt, offset=off)
        off += sz
    assert off <= base + 2 * NIT * 1024, (off, base)
    if l == 1:
        for nm, shape, dt, sz in DQ_BUF_SPECS:
            cx.sb[nm] = cx.nc.alloc_sbuf_tensor_at(nm, shape, dt, offset=off)
            off += sz
        assert off <= base + 2 * NIT * 1024, (off, base)


def emit_ones_f32(cx):
    pg = cx.pg
    t = cx.sb["onescf"]
    pg.op("pool", (lambda e: e.memset(t[:], 1.0)), reads=[], writes=["onescf"],
          tag="onescf")
    t2 = cx.sb["onesrf"]
    pg.op("pool", (lambda e: e.memset(t2[:], 1.0)), reads=[], writes=["onesrf"],
          tag="onesrf")
    t3 = cx.sb["onessq"]
    pg.op("pool", (lambda e: e.memset(t3[:], 1.0)), reads=[], writes=["onessq"],
          tag="onessq")
    t4 = cx.sb["lsel"]
    pg.op("pool", (lambda e: e.memset(t4[:], 0.0)), reads=[], writes=["lsel"],
          tag="lsel0")
    pg.op("pool", (lambda e: e.memset(t4[:, 0:1], 1.0)), reads=["lsel"],
          writes=["lsel"], tag="lsel1")
    pg.op("pool", (lambda e: e.memset(t4[:, 3:4], 1.0)), reads=["lsel"],
          writes=["lsel"], tag="lsel2")
    t5 = cx.sb["onesbf"]
    pg.op("pool", (lambda e: e.memset(t5[:], 1.0)), reads=[], writes=["onesbf"],
          tag="onesbf")


def build_segment_B(l):
    cx = new_ctx()
    nc = cx.nc
    need = {"c_cosd", "c_sinds", "c_masks", "c_identf", "c_onesb", "c_onesc", "c_cvec"}
    if l == 1:
        need.add("c_sel32h")
    decl_common(cx, const_names=need)
    dram(cx, "xn_all", [4 * D, TPC], BF16, kind="ExternalInput")
    dram(cx, "attn_my", [S, TPC], BF16, kind="ExternalOutput")
    dram(cx, "q_scr", [HPC * 128, S], BF16)
    dram(cx, "k_scr", [KVPC * 128, S], BF16)
    dram(cx, "v_scr", [S, KVPC * 128], BF16)
    if l == 0:
        dram(cx, "l0_wq", [D, 1024], F32, kind="ExternalInput")
        dram(cx, "l0_wk", [D, 256], F32, kind="ExternalInput")
        dram(cx, "l0_wv", [D, 256], F32, kind="ExternalInput")
    else:
        dram(cx, "l1_q_cd", [D, 1024], I32, kind="ExternalInput")
        dram(cx, "l1_k_cd", [D, 256], I32, kind="ExternalInput")
        dram(cx, "l1_v_cd", [D, 256], I32, kind="ExternalInput")
        dram(cx, "l1_q_sc", [32, 1024], F32, kind="ExternalInput")
        dram(cx, "l1_k_sc", [32, 256], F32, kind="ExternalInput")
        dram(cx, "l1_v_sc", [32, 256], F32, kind="ExternalInput")
        dram(cx, "wq_deq", [D, 1024], BF16)
        dram(cx, "wk_deq", [D, 256], BF16)
        dram(cx, "wv_deq", [D, 256], BF16)
    decl_segB(cx, l)
    emit_ones_f32(cx)
    wg_keys = [("qk", "wg", 0), ("qk", "wg", 1)]
    if l == 1:
        emit_dequant_qkv_to_dram(cx, "dq")
        cx.pg.fence([("dq", "sc"), ("dq", "scf"), ("dq", "cd", 0), ("dq", "cd", 1)]
                    + [("dq", "wdq", j) for j in range(3)] + wg_keys,
                    tag="dqfence")
    emit_qkv(cx, l, "qk")
    at_keys = (wg_keys + [("at", "q", 0), ("at", "q", 1), ("at", "kbuf", 0),
                          ("at", "kbuf", 1), ("at", "vbuf"), ("at", "lnl"),
                          ("at", "rinv"), ("at", "ast", 0), ("at", "ast", 1)]
               + [("at", "p", j) for j in range(4)])
    cx.pg.fence(at_keys, tag="atfence")
    emit_attn(cx, "at")
    with nc.Block() as block:
        cx.pg.finalize(nc, block)
    nc.finalize()
    return cx





def emit_castw_to_dram(cx, tag, src_name, dst_name):
    pg = cx.pg
    st = pg.stream("wcst", "pool", depth=4)
    s = cx.dr[src_name]
    d = cx.dr[dst_name]
    for it in range(NIT):
        pg.dma(st, (lambda e, i=it: e.dma_start(
            out=d.ap()[i * 128:(i + 1) * 128, :],
            in_=s.ap()[i * 128:(i + 1) * 128, :])),
            reads=[src_name], writes=[dst_name], tag=f"{tag}c{it}")


def emit_scale_hilo(cx, tag, src_name, rows, cols, load_selblk=False):
    pg = cx.pg
    scf = cx.sb["p_scf"]
    hi = cx.sb["p_schi"]
    lo = cx.sb["p_sclo"]
    ld = pg.stream("scl", "pool", depth=2)
    if load_selblk:
        sb_d = cx.dr["c_selblk"]
        pg.dma(ld, (lambda e: e.dma_start(out=cx.sb["selblk"][:], in_=sb_d.ap())),
               reads=["c_selblk"], writes=["selblk"], tag=f"{tag}sld")
    d = cx.dr[src_name]
    pg.dma(ld, (lambda e: e.dma_start(out=scf[0:rows, 0:cols], in_=d.ap())),
           reads=[src_name], writes=[(tag, "scf")], tag=f"{tag}lf")
    pg.dma(ld, (lambda e: e.dma_start(out=hi[0:rows, 0:cols], in_=d.ap())),
           reads=[src_name], writes=[(tag, "schi")], tag=f"{tag}lh")
    pg.op("dve", (lambda e: e.tensor_sub(scf[0:rows, 0:cols], scf[0:rows, 0:cols],
                                         hi[0:rows, 0:cols])),
          reads=[(tag, "scf"), (tag, "schi")], writes=[(tag, "scf")], tag=f"{tag}s")
    pg.op("dve", (lambda e: e.tensor_copy(lo[0:rows, 0:cols], scf[0:rows, 0:cols])),
          reads=[(tag, "scf")], writes=[(tag, "sclo")], tag=f"{tag}c")


def make_dequant_chunks(cx, tag, codes_name, dst_name, kind):
    pg = cx.pg
    selname = "selblk" if kind == "i8" else "sel32h"
    selrows = 128 if kind == "i8" else 32
    bias = -128.0 if kind == "i8" else -8.0
    hi, lo = cx.sb["p_schi"], cx.sb["p_sclo"]
    cdb = cx.sb["p_codes"]
    wdq = [cx.sb[f"p_wdq{j}"] for j in range(3)]
    ldcd = pg.stream("pcd", "sp", depth=4)
    stw = pg.stream("pwd", "act", depth=6)
    s = cx.dr[codes_name]
    d = cx.dr[dst_name]
    state = {"nb": 0}

    def emit_cd(grp):
        it, hh = grp // 2, grp % 2
        half = cdb[:, hh * 2048:(hh + 1) * 2048]
        pg.dma(ldcd, (lambda e, a=half, i=it, h_=hh: e.dma_start(
            out=a, in_=s.ap()[i * 128:(i + 1) * 128,
                              h_ * 2048:(h_ + 1) * 2048])),
            reads=[codes_name], writes=[(tag, "cd", hh)],
            tag=f"{tag}cd{it}_{hh}")

    def step(idx):
        grp, ob = idx // 4, idx % 4
        it, hh = grp // 2, grp % 2
        if ob == 0:
            if grp == 0:
                emit_cd(0)
            if grp + 1 < 2 * NIT:
                emit_cd(grp + 1)
        sel = cx.sb[selname]
        half = cdb[:, hh * 2048:(hh + 1) * 2048]
        rch = (tag, "cd", hh)
        oc = hh * 2048 + ob * 512
        nb = state["nb"]
        wb = wdq[nb % 3]
        rwb = (tag, "wdq", nb % 3)
        ps_sc = cx.ps[7]
        pbk = ("PB", 7)
        state["nb"] += 1
        lhs_sl = sel[:, it * 128:(it + 1) * 128]
        pg.op("pe", (lambda e, ls=lhs_sl, o=oc, p=ps_sc:
                     e.matmul(p[:, 0:512], ls[0:selrows, :],
                              hi[0:selrows, o:o + 512],
                              start=True, stop=False)),
              reads=[(tag, "schi"), selname], writes=[pbk],
              tag=f"{tag}mh{it}_{oc}")
        pg.op("pe", (lambda e, ls=lhs_sl, o=oc, p=ps_sc:
                     e.matmul(p[:, 0:512], ls[0:selrows, :],
                              lo[0:selrows, o:o + 512],
                              start=False, stop=True)),
              reads=[(tag, "sclo"), selname], writes=[pbk],
              tag=f"{tag}ml{it}_{oc}")
        pg.op("dve", (lambda e, a=wb, c_=half, o=ob * 512, p=ps_sc:
                      e.scalar_tensor_tensor(a[:], c_[:, o:o + 512], bias,
                                             p[:, 0:512],
                                             ALU.add, ALU.mult)),
              reads=[rch, pbk], writes=[rwb], tag=f"{tag}dq{it}_{oc}")
        pg.dma(stw, (lambda e, a=wb, i=it, o=oc: e.dma_start(
            out=d.ap()[i * 128:(i + 1) * 128, o:o + 512], in_=a[:])),
            reads=[rwb], writes=[dst_name], tag=f"{tag}st{it}_{oc}")
    return [lambda i=i: step(i) for i in range(NIT * 8)]


def mk_wcb(steps):
    q = list(steps)

    def cb(_=None):
        if q:
            q.pop(0)()
    cb.flush = lambda: [q.pop(0)() for _ in range(len(q))]
    return cb


def emit_oproj(cx, tag, x_in_name, x_out_name, rhs_mode="direct",
               wname="wo_bf", wcast=False):
    pg = cx.pg
    wo = cx.dr[wname]
    xi = cx.dr[x_in_name]
    xo = cx.dr[x_out_name]
    slabs = [cx.sb["wslab0"], cx.sb["wslab1"]]
    x0b = [cx.sb[f"x0b{j}"] for j in range(8)]
    xst = [cx.sb[f"xst{j}"] for j in range(2)]
    ldw = pg.stream("wslc" if wcast else "wsl", "pool" if wcast else "sp",
                    depth=2)
    ldx = pg.stream("xin", "pool", depth=4)
    stx = pg.stream("xout", "act", depth=4)
    banks = [cx.ps[j] for j in range(8)]
    if rhs_mode == "ag":
        at = cx.dr["attn_all"]
        xb = cx.sb["xb_slab"]
        ldr = pg.stream("rhl", "sp", depth=2)

        def _ldall(e, hl):
            if getattr(cx, "rank_sv", None) is None:
                cx.rank_sv = e.value_load(cx.sb["rank_sb"][0:1, 0:1])
                cx.off_cache = {}
            off = cx.off_cache.get(hl)
            if off is None:
                off = e.compute_val(cx.rank_sv + hl * 128 * TPC)
                cx.off_cache[hl] = off
            dst = xb[:].rearrange("p (a h c) -> p a h c", h=HPC, c=TPC)[:, :, hl, :]
            return e.dma_start(
                out=dst,
                in_=bass.AP(at, off, [[TPC, 128], [TPC * TPC, 4], [1, TPC]]))
        for hl in range(HPC):
            pg.dma(ldr, (lambda e, h_=hl: _ldall(e, h_)),
                   reads=["attn_all", ("rank_sb",)],
                   writes=[(tag, "rhsall")], tag=f"{tag}rall{hl}")

        def rhs_ap(hd, tb):
            return cx.sb["xb_slab"][:, hd * TPC + tb * 512:hd * TPC + (tb + 1) * 512]
        rhs_reads = lambda hd: [(tag, "rhsall")]
    else:
        at = cx.dr["attn_tok"]
        rbufs = [cx.sb["rhs0"], cx.sb["rhs1"]]
        ldr = pg.stream("rhl", "sp", depth=3)
    for os_ in range(8):
        wsl = slabs[os_ % 2]
        rw = (tag, "wsl", os_ % 2)
        pg.dma(ldw, (lambda e, w_=wsl, o=os_: e.dma_start(
            out=w_[:].rearrange("p (a c) -> p a c", c=512),
            in_=wo.ap()[:, o * 512:(o + 1) * 512].rearrange(
                "(a p) c -> p a c", p=128))),
            reads=[wname], writes=[rw], tag=f"{tag}w{os_}")
        for j in range(8):
            ot4, tb = j // 2, j % 2
            ot = os_ * 4 + ot4
            xb_ = x0b[j]
            pg.dma(ldx, (lambda e, a=xb_, r0=ot * 128, c0=tb * 512: e.dma_start(
                out=a[:], in_=xi.ap()[r0:r0 + 128, c0:c0 + 512])),
                reads=[("XD", x_in_name, ot, tb)], writes=[(tag, "x0", j)],
                tag=f"{tag}x{os_}_{j}")
        for hd in range(NIT):
            if rhs_mode == "direct":
                rb_ = rbufs[hd % 2]
                rr = (tag, "rhs", hd % 2)
                pg.dma(ldr, (lambda e, a=rb_, h_=hd: e.dma_start(
                    out=a[:], in_=at.ap()[h_ * 128:(h_ + 1) * 128, :])),
                    reads=["attn_tok"], writes=[rr], tag=f"{tag}r{os_}_{hd}")
            for ot4 in range(4):
                for tb in range(2):
                    if rhs_mode == "direct":
                        rh_ap = rb_[:, tb * 512:(tb + 1) * 512]
                        rds = [rw, rr]
                    else:
                        rh_ap = cx.sb["xb_slab"][:, hd * TPC + tb * 512:
                                                 hd * TPC + (tb + 1) * 512]
                        rds = [rw, (tag, "rhsall")]
                    pg.op("pe", (lambda e, p=cx.pp[ot4], w_=wsl, h_=hd, o4=ot4,
                                 t_=tb, s=(hd == 0), z=(hd == NIT - 1), r_=rh_ap:
                                 e.matmul(p[:, t_ * 512:(t_ + 1) * 512],
                                          w_[:, h_ * 512 + o4 * 128:
                                             h_ * 512 + o4 * 128 + 128],
                                          r_, start=s, stop=z)),
                          reads=rds, writes=[("PB", ot4 * 2 + tb)],
                          tag=f"{tag}mm{os_}_{hd}_{ot4}{tb}")
        for j in range(8):
            ot4, tb = j // 2, j % 2
            ot = os_ * 4 + ot4
            pb = banks[j]
            xb_ = x0b[j]
            xs = xst[j % 2]
            rxs = (tag, "xst", j % 2)
            pg.op("dve", (lambda e, p=pb, a=xs, b=xb_:
                          e.tensor_add(a[:], p[:], b[:])),
                  reads=[("PB", j), (tag, "x0", j)], writes=[rxs],
                  tag=f"{tag}ev{os_}_{j}")
            pg.dma(stx, (lambda e, a=xs, r0=ot * 128, c0=tb * 512: e.dma_start(
                out=xo.ap()[r0:r0 + 128, c0:c0 + 512], in_=a[:])),
                reads=[rxs], writes=[("XD", x_out_name, ot, tb)],
                tag=f"{tag}sx{os_}_{j}")


def emit_gates(cx, tag):
    pg = cx.pg
    gsl = cx.sb["gates_slab"]
    xb = cx.sb["xb_slab"]
    qk = cx.sb["qk_sb"]
    hsb = cx.sb["h_sb"]
    prod = cx.sb["prod_sb"]
    sc8 = cx.sb["sc8"]
    g = cx.sb["g_sb"]
    gslab = cx.sb["g_slab"]
    identf = cx.sb["identf"]
    ld = pg.stream("gld", "pool", depth=2)
    gates = cx.dr["l1f_gates"]
    pg.dma(ld, (lambda e: e.dma_start(
        out=gsl[:].rearrange("p (a c) -> p a c", c=640),
        in_=gates.ap().rearrange("(a p) c -> p a c", p=128))),
        reads=["l1f_gates"], writes=[(tag, "gsl")], tag=f"{tag}ld")
    ps1, ps2 = cx.ps[0], cx.ps[1]
    ps_tr = cx.ps[2]
    for tt in range(8):
        lhs_of = tt * 128
        for it in range(NIT):
            lhs = xb[:, it * 1024 + lhs_of:it * 1024 + lhs_of + 128]
            pg.op("pe", (lambda e, a=lhs, i=it, s=(it == 0), z=(it == NIT - 1):
                         e.matmul(ps1[:], a, gsl[:, i * 640:i * 640 + 512],
                                  start=s, stop=z)),
                  reads=[("xb",), (tag, "gsl")], writes=[("PB", 0)],
                  tag=f"{tag}m1{tt}_{it}")
            pg.op("pe", (lambda e, a=lhs, i=it, s=(it == 0), z=(it == NIT - 1):
                         e.matmul(ps2[:, 0:128], a,
                                  gsl[:, i * 640 + 512:i * 640 + 640],
                                  start=s, stop=z)),
                  reads=[("xb",), (tag, "gsl")], writes=[("PB", 1)],
                  tag=f"{tag}m2{tt}_{it}")
        pg.op("dve", (lambda e: e.tensor_copy(qk[:], ps1[:])),
              reads=[("PB", 0)], writes=[(tag, "qk")], tag=f"{tag}eq{tt}")
        pg.op("dve", (lambda e: e.tensor_copy(hsb[:], ps2[:, 0:128])),
              reads=[("PB", 1)], writes=[(tag, "h")], tag=f"{tag}eh{tt}")
        pg.op("dve", (lambda e: e.tensor_mul(prod[:], qk[:, 0:256], qk[:, 256:512])),
              reads=[(tag, "qk")], writes=[(tag, "prod")], tag=f"{tag}pr{tt}")
        pg.op("dve", (lambda e: e.reduce_sum(
            sc8[:, 0:8].rearrange("p (e o) -> p e o", o=1),
            prod[:].rearrange("p (e d) -> p e d", d=32), axis=AX.X)),
              reads=[(tag, "prod")], writes=[(tag, "sc8")], tag=f"{tag}rs{tt}")
        pg.op("dve", (lambda e: e.reduce_max(sc8[:, 8:9], sc8[:, 0:8], axis=AX.X)),
              reads=[(tag, "sc8")], writes=[(tag, "sc8")], tag=f"{tag}mx{tt}")
        pg.op("dve", (lambda e: e.tensor_scalar(sc8[:, 9:10], sc8[:, 8:9],
                                                -1.0 / math.sqrt(32.0), None,
                                                ALU.mult)),
              reads=[(tag, "sc8")], writes=[(tag, "sc8")], tag=f"{tag}nb{tt}")
        pg.op("act", (lambda e: e.activation(sc8[:, 16:24], sc8[:, 0:8], AF.Exp,
                                             bias=sc8[:, 9:10], scale=cv(cx, 4))),
              reads=[(tag, "sc8"), "cvec"], writes=[(tag, "sc8")], tag=f"{tag}ex{tt}")
        pg.op("dve", (lambda e: e.reduce_sum(sc8[:, 10:11], sc8[:, 16:24], axis=AX.X)),
              reads=[(tag, "sc8")], writes=[(tag, "sc8")], tag=f"{tag}sm{tt}")
        pg.op("dve", (lambda e: e.reciprocal(sc8[:, 11:12], sc8[:, 10:11])),
              reads=[(tag, "sc8")], writes=[(tag, "sc8")], tag=f"{tag}rc{tt}")
        pg.op("dve", (lambda e: e.tensor_scalar(sc8[:, 16:24], sc8[:, 16:24],
                                                sc8[:, 11:12], None, ALU.mult)),
              reads=[(tag, "sc8")], writes=[(tag, "sc8")], tag=f"{tag}w8{tt}")
        def gmul(e):
            h3 = hsb[:].rearrange("p (e r) -> p e r", r=16)
            w3 = sc8[:, 16:24].rearrange("p (e o) -> p e o", o=1)
            g3 = g[:].rearrange("p (e r) -> p e r", r=16)
            h3b, w3b = bass.broadcast_tensor_aps(h3, w3)
            return e.tensor_tensor(g3, h3b, w3b, ALU.mult)
        pg.op("dve", gmul, reads=[(tag, "h"), (tag, "sc8")], writes=[(tag, "g")],
              tag=f"{tag}g{tt}")
        pg.op("pe", (lambda e: e.transpose(ps_tr[:, 0:128], g[:], identf[:])),
              reads=[(tag, "g"), "identf"], writes=[("PB", 2)], tag=f"{tag}tr{tt}")
        pg.op("dve", (lambda e, o=tt * 128: e.tensor_copy(gslab[:, o:o + 128],
                                                          ps_tr[:, 0:128])),
              reads=[("PB", 2)], writes=[(tag, "gslab")], tag=f"{tag}gc{tt}")


def emit_ffn(cx, l, tag, x_in_name, x_out_name, wname="w1_bf", wcast=False):
    pg = cx.pg
    w1 = cx.dr[wname]
    xi = cx.dr[x_in_name]
    xo = cx.dr[x_out_name]
    xb = cx.sb["xb_slab"]
    slabs = [cx.sb["wslab0"], cx.sb["wslab1"]]
    x0b = [cx.sb[f"x0b{j}"] for j in range(8)]
    xst = [cx.sb[f"xst{j}"] for j in range(2)]
    ldw = pg.stream("wslc" if wcast else "wsl", "pool" if wcast else "sp",
                    depth=2)
    ldx = pg.stream("xin", "pool", depth=4)
    stx = pg.stream("xout", "act", depth=4)
    banks = [cx.ps[j] for j in range(8)]
    if l == 1:
        bsb = cx.sb["b_sb"]
        bmt = cx.sb["bmt_sb"]
        gslab = cx.sb["g_slab"]
    for os_ in range(8):
        wsl = slabs[os_ % 2]
        rw = (tag, "wsl", os_ % 2)
        pg.dma(ldw, (lambda e, w_=wsl, o=os_: e.dma_start(
            out=w_[:].rearrange("p (a c) -> p a c", c=512),
            in_=w1.ap()[:, o * 512:(o + 1) * 512].rearrange(
                "(a p) c -> p a c", p=128))),
            reads=[wname], writes=[rw], tag=f"{tag}w{os_}")
        for j in range(8):
            ot4, tb = j // 2, j % 2
            ot = os_ * 4 + ot4
            xb_ = x0b[j]
            pg.dma(ldx, (lambda e, a=xb_, r0=ot * 128, c0=tb * 512: e.dma_start(
                out=a[:], in_=xi.ap()[r0:r0 + 128, c0:c0 + 512])),
                reads=[("XD", x_in_name, ot, tb)], writes=[(tag, "x0", j)],
                tag=f"{tag}x{os_}_{j}")
        for it in range(NIT):
            for ot4 in range(4):
                for tb in range(2):
                    last = (it == NIT - 1) and l == 0
                    pg.op("pe", (lambda e, p=cx.pp[ot4], w_=wsl, i=it, o4=ot4,
                                 t_=tb, s=(it == 0), z=last:
                                 e.matmul(p[:, t_ * 512:(t_ + 1) * 512],
                                          w_[:, i * 512 + o4 * 128:
                                             i * 512 + o4 * 128 + 128],
                                          xb[:, i * 1024 + t_ * 512:
                                             i * 1024 + (t_ + 1) * 512],
                                          start=s, stop=z)),
                          reads=[rw, ("xb",)], writes=[("PB", ot4 * 2 + tb)],
                          tag=f"{tag}mm{os_}_{it}_{ot4}{tb}")
        if l == 1:
            for ot4 in range(4):
                for tb in range(2):
                    pb = banks[ot4 * 2 + tb]
                    ot = os_ * 4 + ot4
                    pg.op("pe", (lambda e, p=pb, o=ot * 128, t_=tb:
                                 e.matmul(p[:], bmt[:, o:o + 128],
                                          gslab[:, t_ * 512:(t_ + 1) * 512],
                                          start=False, stop=True)),
                          reads=[("bmt",), ("gt", "gslab")],
                          writes=[("PB", ot4 * 2 + tb)],
                          tag=f"{tag}bm{os_}_{ot4}{tb}")
        for j in range(8):
            ot4, tb = j // 2, j % 2
            ot = os_ * 4 + ot4
            pb = banks[j]
            xb_ = x0b[j]
            xs = xst[j % 2]
            rxs = (tag, "xst", j % 2)
            if l == 1:
                pg.op("dve", (lambda e, p=pb, a=xs, b=xb_, o=ot:
                              e.scalar_tensor_tensor(a[:], p[:], bsb[:, o:o + 1],
                                                     b[:], ALU.add, ALU.add)),
                      reads=[("PB", j), (tag, "x0", j), ("bsb",)],
                      writes=[rxs], tag=f"{tag}ev{os_}_{j}")
            else:
                pg.op("dve", (lambda e, p=pb, a=xs, b=xb_:
                              e.tensor_add(a[:], p[:], b[:])),
                      reads=[("PB", j), (tag, "x0", j)], writes=[rxs],
                      tag=f"{tag}ev{os_}_{j}")
            pg.dma(stx, (lambda e, a=xs, r0=ot * 128, c0=tb * 512: e.dma_start(
                out=xo.ap()[r0:r0 + 128, c0:c0 + 512], in_=a[:])),
                reads=[rxs], writes=[("XD", x_out_name, ot, tb)],
                tag=f"{tag}sx{os_}_{j}")


PREP_SPECS = [
    ("p_codes", [128, 4096], I32, 16384),
    ("p_scf", [128, 4096], F32, 16384),
    ("p_schi", [128, 4096], BF16, 8192),
    ("p_sclo", [128, 4096], BF16, 8192),
    ("p_wdq0", [128, 512], BF16, 1024), ("p_wdq1", [128, 512], BF16, 1024),
    ("p_wdq2", [128, 512], BF16, 1024),
]
GATE_SPECS = [
    ("gates_slab", [128, 32 * 640], BF16, 40960),
    ("qk_sb", [128, 512], F32, 2048),
    ("h_sb", [128, 128], F32, 512),
    ("prod_sb", [128, 256], F32, 1024),
    ("g_sb", [128, 128], F32, 512),
]
PREP_KEYS = [("pw", "cd", 0), ("pw", "cd", 1), ("pw", "scf"), ("pw", "schi"),
             ("pw", "sclo")] + [("pw", "wdq", j) for j in range(3)]
GATE_KEYS = [("gt", "gsl"), ("gt", "qk"), ("gt", "h"), ("gt", "prod"),
             ("gt", "sc8"), ("gt", "g")]


def decl_segC(cx, l):
    if l == 0:
        sb_alloc(cx, "selblk", [128, 4096], BF16)
    sb_alloc(cx, "xb_slab", [128, NIT * 1024], BF16)
    sb_alloc(cx, "wslab0", [128, NIT * 512], BF16)
    sb_alloc(cx, "wslab1", [128, NIT * 512], BF16)
    sb_alloc(cx, "rhs0", [128, 1024], BF16)
    sb_alloc(cx, "rhs1", [128, 1024], BF16)
    for j in range(8):
        sb_alloc(cx, f"x0b{j}", [128, 512], F32)
    for j in range(2):
        sb_alloc(cx, f"xst{j}", [128, 512], F32)
    sb_alloc(cx, "xtile0", [128, 1024], F32)
    sb_alloc(cx, "xtile1", [128, 1024], F32)
    sb_alloc(cx, "sq0", [128, 1024], BF16)
    sb_alloc(cx, "sq1", [128, 1024], BF16)
    sb_alloc(cx, "stats", [1, 3072], F32)
    sb_alloc(cx, "srow", [1, 2048], BF16)
    sb_alloc(cx, "bct", [128, 2048], BF16)
    if l == 1:
        sb_alloc(cx, "b_sb", [128, 32], F32)
        sb_alloc(cx, "bmt_sb", [128, 4096], BF16)
        sb_alloc(cx, "g_slab", [128, 1024], BF16)
        sb_alloc(cx, "sc8", [128, 32], F32)
    base = cx.nc.lookup_mloc(cx.sb["wslab0"]).addr
    off = base
    for nm, shape, dt, sz in PREP_SPECS:
        cx.sb[nm] = cx.nc.alloc_sbuf_tensor_at(nm, shape, dt, offset=off)
        off += sz
    assert off <= base + 2 * NIT * 1024, (off, base)
    if l == 1:
        off = base
        for nm, shape, dt, sz in GATE_SPECS:
            if nm in cx.sb and nm in ("g_slab",):
                continue
            cx.sb[nm] = cx.nc.alloc_sbuf_tensor_at(nm, shape, dt, offset=off)
            off += sz
        assert off <= base + 2 * NIT * 1024, (off, base)


WSLAB_KEYS = [("op", "wsl", 0), ("op", "wsl", 1), ("ff", "wsl", 0),
              ("ff", "wsl", 1)]


def emit_segC_body(cx, l):
    pg = cx.pg
    if l == 0:
        emit_castw_to_dram(cx, "cwo", "l0_wo", "wo_bf")
        emit_scale_hilo(cx, "pw", "l0f_sc", 128, 4096)
        emit_dequant_big(cx, "pw", "l0f_cd", "w1_bf", "i8")
    else:
        emit_scale_hilo(cx, "pw", "l1_o_sc", 32, 4096)
        emit_dequant_big(cx, "pw", "l1_o_cd", "wo_bf", "f4")
        emit_castw_to_dram(cx, "cw1", "l1f_WT", "w1_bf")
    pg.fence(PREP_KEYS + WSLAB_KEYS, tag="prepfence")
    if l == 1:
        ldc = pg.stream("cld2", "pool", depth=2)
        b_d = cx.dr["l1f_b"]
        bm_d = cx.dr["l1f_BmT"]
        pg.dma(ldc, (lambda e: e.dma_start(out=cx.sb["b_sb"][:], in_=b_d.ap())),
               reads=["l1f_b"], writes=[("bsb",)], tag="ldb")
        pg.dma(ldc, (lambda e: e.dma_start(out=cx.sb["bmt_sb"][:], in_=bm_d.ap())),
               reads=["l1f_BmT"], writes=[("bmt",)], tag="ldbmt")
    emit_oproj(cx, "op", "x_in", "x_mid")
    emit_ln_standalone2(cx, "x_mid", None, "ln1")
    if l == 1:
        pg.fence(WSLAB_KEYS + GATE_KEYS, tag="gatefence")
        emit_gates(cx, "gt")
        pg.fence(WSLAB_KEYS + GATE_KEYS, tag="gatefence2")
    emit_ffn(cx, l, "ff", "x_mid", "x_out")
    if l == 0:
        emit_ln_standalone2(cx, "x_out", "xn_out", "ln2")


def emit_ln_standalone2(cx, x_src_name, xn_dst_name, tag):
    pg = cx.pg
    x_src = cx.dr[x_src_name]
    xn_dst = cx.dr[xn_dst_name] if xn_dst_name else None
    ld = pg.stream("xload", "sp", depth=6)
    st_out = pg.stream("agout", "act", depth=6)
    xb = cx.sb["xb_slab"]
    xts = [cx.sb["xtile0"], cx.sb["xtile1"]]
    sqs = [cx.sb["sq0"], cx.sb["sq1"]]
    for it in range(NIT):
        xt = xts[it % 2]
        sq_t = sqs[it % 2]
        rx = ("LN", "xt", it % 2)
        rsq = ("LN", "sq", it % 2)
        pg.dma(ld, (lambda e, a=xt, b=x_src, i=it:
                    e.dma_start(out=a[:], in_=b.ap()[i * 128:(i + 1) * 128, :])),
               reads=[("XD", x_src_name, it, 0), ("XD", x_src_name, it, 1)],
               writes=[rx], tag=f"{tag}ldx{it}")
        xbv = xb[:, it * 1024:(it + 1) * 1024]
        pg.op("dve", (lambda e, a=xbv, b=xt: e.tensor_copy(a, b[:])),
              reads=[rx], writes=[("xb",)], tag=f"{tag}cast{it}")
        pg.op("act", (lambda e, a=sq_t, b=xt: e.activation(a[:], b[:], AF.Square)),
              reads=[rx], writes=[rsq], tag=f"{tag}sq{it}")
        emit_ln_tile_stats(cx, tag, it, ("xb",), xbv, rsq, sq_t[:],
                           first=(it == 0), last=(it == NIT - 1))
    rbct = emit_ln_finalize(cx, tag)
    for it in range(NIT):
        xbv = xb[:, it * 1024:(it + 1) * 1024]
        emit_ln_apply(cx, tag, it, xbv, ("xb",), rbct, xn_dst_name, xn_dst, st_out)


def build_segment_C(l):
    cx = new_ctx()
    nc = cx.nc
    need = {"c_identf", "c_onesb", "c_onesc", "c_cvec"}
    need.add("c_selblk" if l == 0 else "c_sel32h")
    decl_common(cx, const_names=need)
    dram(cx, "attn_tok", [S, TPC], BF16, kind="ExternalInput")
    dram(cx, "x_in", [D, TPC], F32, kind="ExternalInput")
    dram(cx, "x_mid", [D, TPC], F32)
    dram(cx, "x_out", [D, TPC], F32, kind="ExternalOutput")
    dram(cx, "wo_bf", [D, D], BF16)
    dram(cx, "w1_bf", [D, D], BF16)
    if l == 0:
        dram(cx, "xn_out", [D, TPC], BF16, kind="ExternalOutput")
        dram(cx, "l0_wo", [D, D], F32, kind="ExternalInput")
        dram(cx, "l0f_cd", [D, D], I32, kind="ExternalInput")
        dram(cx, "l0f_sc", [128, D], F32, kind="ExternalInput")
    else:
        dram(cx, "l1_o_cd", [D, D], I32, kind="ExternalInput")
        dram(cx, "l1_o_sc", [32, D], F32, kind="ExternalInput")
        dram(cx, "l1f_WT", [D, D], F32, kind="ExternalInput")
        dram(cx, "l1f_b", [128, 32], F32, kind="ExternalInput")
        dram(cx, "l1f_gates", [D, 640], F32, kind="ExternalInput")
        dram(cx, "l1f_BmT", [128, D], F32, kind="ExternalInput")
    decl_segC(cx, l)
    emit_segC_body(cx, l)
    with nc.Block() as block:
        cx.pg.finalize(nc, block)
    nc.finalize()
    return cx







P1K = ([("dq", "sc"), ("dq", "cd", 0), ("dq", "cd", 1)]
       + [("dq", "wdq", j) for j in range(3)])
P2K = ([("at", "q", 0), ("at", "q", 1), ("at", "kbuf", 0), ("at", "kbuf", 1),
        ("at", "vbuf"),
        ("at", "lnl"), ("at", "rinv"), ("at", "ast", 0), ("at", "ast", 1),
        "masks"]
       + [("at", "p", j) for j in range(4)])
P3K = PREP_KEYS + ["selblk"]
P4K = [("LN", "xt", 0), ("LN", "xt", 1), ("LN", "sq", 0), ("LN", "sq", 1),
       ("LN", "stats"), ("LN", "srow"), ("LN", "bct")]
P5K = GATE_KEYS
WGK = [("qk", "wu", 0), ("qk", "wu", 1), ("qk", "wu", 2),
       ("op", "wsl", 0), ("op", "wsl", 1),
       ("ff", "wsl", 0), ("ff", "wsl", 1)]
FENCE_ALL = (P1K + P2K + P3K + P4K + P5K + WGK
             + [("qk", "xn"), ("xb",), ("op", "rhsall")])

RG = [[0, 1, 2, 3], [4, 5, 6, 7]]

XB_PREP_SPECS = PREP_SPECS + [("selblk", [128, 4096], BF16, 8192)]

POOL_SETS = {
    "P6": [("wu0", [128, 4096], BF16, 8192), ("wu1", [128, 4096], BF16, 8192),
           ("wu2", [128, 4096], BF16, 8192)],
    "P1": DQ_BUF_SPECS,
    "P2": ATTN_BUF_SPECS + [("masks", [128, 4096], BF16, 8192)],
    "P4": [("xtile0", [128, 1024], F32, 4096), ("xtile1", [128, 1024], F32, 4096),
           ("sq0", [128, 1024], BF16, 2048), ("sq1", [128, 1024], BF16, 2048),
           ("stats", [1, 3072], F32, 12288), ("srow", [1, 2048], BF16, 4096),
           ("bct", [128, 2048], BF16, 4096)],
    "P5": GATE_SPECS,
}


def decl_fused(cx):
    nc = cx.nc
    decl_common(cx, const_names={"c_cosd", "c_sinds", "c_masks", "c_selblk",
                                 "c_sel32h", "c_identf", "c_onesb", "c_onesc",
                                 "c_cvec"})
    dram(cx, "x_fm", [D, TPC], F32, kind="ExternalInput")
    dram(cx, "rank_base", [1, 1], I32, kind="ExternalInput")
    dram(cx, "l0_wq", [D, 1024], F32, kind="ExternalInput")
    dram(cx, "l0_wk", [D, 256], F32, kind="ExternalInput")
    dram(cx, "l0_wv", [D, 256], F32, kind="ExternalInput")
    dram(cx, "l0_wo", [D, D], F32, kind="ExternalInput")
    dram(cx, "l0f_cd", [D, D], I32, kind="ExternalInput")
    dram(cx, "l0f_sc", [128, D], F32, kind="ExternalInput")
    dram(cx, "l1_q_cd", [D, 1024], I32, kind="ExternalInput")
    dram(cx, "l1_k_cd", [D, 256], I32, kind="ExternalInput")
    dram(cx, "l1_v_cd", [D, 256], I32, kind="ExternalInput")
    dram(cx, "l1_q_sc", [32, 1024], F32, kind="ExternalInput")
    dram(cx, "l1_k_sc", [32, 256], F32, kind="ExternalInput")
    dram(cx, "l1_v_sc", [32, 256], F32, kind="ExternalInput")
    dram(cx, "l1_o_cd", [D, D], I32, kind="ExternalInput")
    dram(cx, "l1_o_sc", [32, D], F32, kind="ExternalInput")
    dram(cx, "l1f_WT", [D, D], F32, kind="ExternalInput")
    dram(cx, "l1f_b", [128, 32], F32, kind="ExternalInput")
    dram(cx, "l1f_gates", [D, 640], F32, kind="ExternalInput")
    dram(cx, "l1f_BmT", [128, D], F32, kind="ExternalInput")
    dram(cx, "out_fm", [D, TPC], F32, kind="ExternalOutput")
    dram(cx, "ag_in", [D, TPC], BF16)
    dram(cx, "xn_all", [4 * D, TPC], BF16)
    dram(cx, "a2a_in", [S, TPC], BF16)
    dram(cx, "attn_all", [4 * S, TPC], BF16)
    dram(cx, "q_scr", [HPC * 128, S], BF16)
    dram(cx, "k_scr", [KVPC * 128, S], BF16)
    dram(cx, "v_scr", [S, KVPC * 128], BF16)
    dram(cx, "wq_deq", [D, 1024], BF16)
    dram(cx, "wk_deq", [D, 256], BF16)
    dram(cx, "wv_deq", [D, 256], BF16)
    dram(cx, "wo_bf", [D, D], BF16)
    dram(cx, "w1_bf", [D, D], BF16)
    dram(cx, "x_mid", [D, TPC], F32)
    dram(cx, "x2_scr", [D, TPC], F32)
    sb_alloc(cx, "xb_slab", [128, NIT * 1024], BF16)
    sb_alloc(cx, "wgrp0", [128, NIT * 512], BF16)
    sb_alloc(cx, "wgrp1", [128, NIT * 512], BF16)
    cx.sb["wslab0"] = cx.sb["wgrp0"]
    cx.sb["wslab1"] = cx.sb["wgrp1"]
    for si in range(2):
        sb_alloc(cx, f"qstage{si}", [128, 512], F32)
        sb_alloc(cx, f"qswap{si}", [128, 512], F32)
        sb_alloc(cx, f"qrot{si}", [128, 512], BF16)
        sb_alloc(cx, f"utmp{si}", [128, 1024], F32)
    sb_alloc(cx, "vstage", [128, 512], F32)
    sb_alloc(cx, "vbf", [128, 512], BF16)
    sb_alloc(cx, "onescf", [128, 1], F32)
    sb_alloc(cx, "onesrf", [1, 128], F32)
    sb_alloc(cx, "onessq", [128, 128], F32)
    sb_alloc(cx, "lsel", [128, 4], BF16)
    sb_alloc(cx, "onesbf", [33, 128], F32)
    for j in range(8):
        sb_alloc(cx, f"x0b{j}", [128, 512], F32)
    for j in range(2):
        sb_alloc(cx, f"xst{j}", [128, 512], F32)
    sb_alloc(cx, "b_sb", [128, 32], F32)
    sb_alloc(cx, "bmt_sb", [128, 4096], BF16)
    sb_alloc(cx, "g_slab", [128, 1024], BF16)
    sb_alloc(cx, "sc8", [128, 32], F32)
    sb_alloc(cx, "rank_sb", [1, 2], I32)
    base = nc.lookup_mloc(cx.sb["wgrp0"]).addr
    for pname, specs in POOL_SETS.items():
        off = base + (32768 if pname == "P1" else 0)
        for nm, shape, dt, sz in specs:
            cx.sb[nm] = nc.alloc_sbuf_tensor_at(nm, shape, dt, offset=off)
            off += sz
        assert off <= base + 2 * NIT * 1024, (off, base)
    xbase = nc.lookup_mloc(cx.sb["xb_slab"]).addr
    off = xbase
    for nm, shape, dt, sz in XB_PREP_SPECS:
        cx.sb[nm] = nc.alloc_sbuf_tensor_at(nm, shape, dt, offset=off)
        off += sz
    assert off <= xbase + NIT * 2048, (off, xbase)


def build_fused():
    cx = new_ctx()
    nc = cx.nc
    pg = cx.pg
    decl_fused(cx)
    emit_ones_f32(cx)
    rkld = pg.stream("rkld", "sp", depth=1)
    rb_d = cx.dr["rank_base"]
    pg.dma(rkld, (lambda e: e.dma_start(out=cx.sb["rank_sb"][0:1, 0:1],
                                        in_=rb_d.ap())),
           reads=["rank_base"], writes=[("rank_sb",)], tag="ldrank")

    def fence():
        pg.fence(FENCE_ALL, tag="F")

    def agx():
        ag_in, xn_all = cx.dr["ag_in"], cx.dr["xn_all"]
        for f in range(4):
            pg.cc((lambda e, f_=f: e.collective_compute(
                "AllGather", ALU.bypass, replica_groups=RG,
                ins=[ag_in.ap()[f_ * 1024:(f_ + 1) * 1024, :].opt()],
                outs=[xn_all.ap()[f_ * D:(f_ + 1) * D, :].opt()])),
                reads=[("XD", "ag_in", it, tb) for it in range(8 * f, 8 * f + 8)
                       for tb in range(2)],
                writes=[("xn_all", f)], tag=f"agx{f}")

    def aga(jb):
        a2a_in, attn_all = cx.dr["a2a_in"], cx.dr["attn_all"]
        pg.cc((lambda e: e.collective_compute(
            "AllGather", ALU.bypass, replica_groups=RG,
            ins=[a2a_in.ap()[jb * 1024:(jb + 1) * 1024, :].opt()],
            outs=[attn_all.ap()[jb * S:(jb + 1) * S, :].opt()])),
            reads=[("a2a", jb)], writes=["attn_all"], tag=f"aga{jb}")

    emit_ln_standalone2(cx, "x_fm", "ag_in", "lnA")
    agx()
    fence()
    emit_qkv(cx, 0, "qk")
    fence()
    emit_scale_hilo(cx, "pw", "l0f_sc", 128, 4096, load_selblk=True)
    wcb = mk_wcb(make_dequant_chunks(cx, "pw", "l0f_cd", "w1_bf", "i8"))
    emit_attn(cx, "at", dst_name="a2a_in", chunk_cc=aga, work_cb=wcb)
    wcb.flush()
    fence()
    emit_oproj(cx, "op", "x_fm", "x_mid", rhs_mode="ag",
               wname="l0_wo", wcast=True)
    fence()
    emit_ln_standalone2(cx, "x_mid", None, "ln1")
    fence()
    emit_ffn(cx, 0, "ff", "x_mid", "x2_scr")
    fence()
    emit_ln_standalone2(cx, "x2_scr", "ag_in", "ln2")
    emit_dequant_qkv_to_dram(cx, "dq")
    agx()
    fence()
    emit_qkv(cx, 1, "qk")
    fence()
    emit_scale_hilo(cx, "pw", "l1_o_sc", 32, 4096)
    wcb = mk_wcb(make_dequant_chunks(cx, "pw", "l1_o_cd", "wo_bf", "f4"))
    emit_attn(cx, "at", dst_name="a2a_in", chunk_cc=aga, work_cb=wcb)
    wcb.flush()
    fence()
    emit_oproj(cx, "op", "x2_scr", "x_mid", rhs_mode="ag")
    fence()
    emit_ln_standalone2(cx, "x_mid", None, "ln3")
    fence()
    ldc = pg.stream("cld2", "pool", depth=2)
    b_d = cx.dr["l1f_b"]
    bm_d = cx.dr["l1f_BmT"]
    pg.dma(ldc, (lambda e: e.dma_start(out=cx.sb["b_sb"][:], in_=b_d.ap())),
           reads=["l1f_b"], writes=[("bsb",)], tag="ldb")
    pg.dma(ldc, (lambda e: e.dma_start(out=cx.sb["bmt_sb"][:], in_=bm_d.ap())),
           reads=["l1f_BmT"], writes=[("bmt",)], tag="ldbmt")
    emit_gates(cx, "gt")
    fence()
    emit_ffn(cx, 1, "ff", "x_mid", "out_fm", wname="l1f_WT", wcast=True)
    with nc.Block() as block:
        pg.finalize(nc, block)
    nc.finalize()
    return cx


def kernel_fused(inputs):
    cc = np.ascontiguousarray
    x0 = inputs["L0_attn_hidden_states"]
    CON = _consts()
    cx = _build("FUSED", build_fused)
    A_ = inputs["L1_ffn_A"]
    Bm = inputs["L1_ffn_Bm"]
    gates = cc(np.concatenate([inputs["L1_ffn_Wq"].T, inputs["L1_ffn_Wk"].T,
                               np.asarray(A_).reshape(128, D).T], axis=1))
    BmT = cc(np.asarray(Bm).transpose(0, 2, 1).reshape(128, D))
    b_sb = cc(np.asarray(inputs["L1_ffn_b"]).reshape(32, 128).T)
    w1T = cc(np.asarray(inputs["L1_ffn_W"]).T)
    l0f_cd = cc(np.asarray(inputs["L0_ffn_w_q"]).T)
    l0f_sc = cc(np.asarray(inputs["L0_ffn_scales"]).T)
    Wq, Wk, Wv = inputs["L0_attn_Wq"], inputs["L0_attn_Wk"], inputs["L0_attn_Wv"]
    qc, kc, vc = (inputs["L1_attn_q_codes"], inputs["L1_attn_k_codes"],
                  inputs["L1_attn_v_codes"])
    qs, ks, vs = (inputs["L1_attn_q_scales"], inputs["L1_attn_k_scales"],
                  inputs["L1_attn_v_scales"])
    maps = []
    for c in range(NC):
        b, r = c // 4, c % 4
        m = {n: CON[n] for n in ["c_cosd", "c_sinds", "c_masks", "c_selblk",
                                 "c_sel32h", "c_identf", "c_onesb", "c_onesc",
                                 "c_cvec"]}
        m.update(
            x_fm=cc(x0[b, r * TPC:(r + 1) * TPC, :].T),
            rank_base=np.array([[r * 4 * TPC * TPC]], dtype=np.int32),
            l0_wq=cc(Wq[:, r * 1024:(r + 1) * 1024]),
            l0_wk=cc(Wk[:, r * 256:(r + 1) * 256]),
            l0_wv=cc(Wv[:, r * 256:(r + 1) * 256]),
            l0_wo=inputs["L0_attn_Wo"], l0f_cd=l0f_cd, l0f_sc=l0f_sc,
            l1_q_cd=cc(qc[:, r * 1024:(r + 1) * 1024]),
            l1_k_cd=cc(kc[:, r * 256:(r + 1) * 256]),
            l1_v_cd=cc(vc[:, r * 256:(r + 1) * 256]),
            l1_q_sc=cc(qs[:, r * 1024:(r + 1) * 1024]),
            l1_k_sc=cc(ks[:, r * 256:(r + 1) * 256]),
            l1_v_sc=cc(vs[:, r * 256:(r + 1) * 256]),
            l1_o_cd=inputs["L1_attn_o_codes"],
            l1_o_sc=inputs["L1_attn_o_scales"],
            l1f_WT=w1T, l1f_b=b_sb, l1f_gates=gates, l1f_BmT=BmT,
        )
        maps.append(m)
    res = _run(cx, maps)
    out = np.empty((2, S, D), dtype=np.float32)
    for c in range(NC):
        b, r = c // 4, c % 4
        out[b, r * TPC:(r + 1) * TPC, :] = res[c]["out_fm"].T
    return out


_CONSTS = None


def _consts():
    global _CONSTS
    if _CONSTS is None:
        _CONSTS = make_consts()
    return _CONSTS


def _const_map(names):
    c = _consts()
    return {n: c[n] for n in names}


_BUILDS = {}


def _build(name, fn, *args):
    if name not in _BUILDS:
        _BUILDS[name] = fn(*args)
    return _BUILDS[name]


def _run(cx, in_maps):
    res = run_bass_kernel_spmd(cx.nc, in_maps, core_ids=list(range(NC)))
    return res.results


def kernel(**inputs):
    return kernel_fused(inputs)


def kernel_unfused(**inputs):
    cc = np.ascontiguousarray
    x0 = inputs["L0_attn_hidden_states"]
    conA = _const_map(["c_onesb", "c_onesc", "c_cvec"])
    conB0 = _const_map(["c_cosd", "c_sinds", "c_masks", "c_identf", "c_onesb",
                        "c_onesc", "c_cvec"])
    conB1 = dict(conB0, **_const_map(["c_sel32h"]))
    conC0 = _const_map(["c_identf", "c_onesb", "c_onesc", "c_cvec", "c_selblk"])
    conC1 = _const_map(["c_identf", "c_onesb", "c_onesc", "c_cvec", "c_sel32h"])

    def core_bt(c):
        return c // 4, c % 4

    x_fm = []
    for c in range(NC):
        b, r = core_bt(c)
        x_fm.append(cc(x0[b, r * TPC:(r + 1) * TPC, :].T))

    cxA = _build("A", build_segment_A)
    resA = _run(cxA, [dict(conA, x_fm=x_fm[c]) for c in range(NC)])
    xn0 = [resA[c]["xn0"] for c in range(NC)]

    def gather_xn(xn):
        xn_all = []
        for c in range(NC):
            b, r = core_bt(c)
            xn_all.append(cc(np.concatenate([xn[4 * b + i] for i in range(4)],
                                            axis=0)))
        return xn_all

    def a2a(attn_my):
        attn_tok = []
        for c in range(NC):
            b, r = core_bt(c)
            attn_tok.append(cc(np.concatenate(
                [attn_my[4 * b + i][r * TPC:(r + 1) * TPC, :] for i in range(4)],
                axis=0)))
        return attn_tok

    Wq, Wk, Wv = inputs["L0_attn_Wq"], inputs["L0_attn_Wk"], inputs["L0_attn_Wv"]
    cxB0 = _build("B0", build_segment_B, 0)
    xn_all = gather_xn(xn0)
    maps = []
    for c in range(NC):
        b, r = core_bt(c)
        maps.append(dict(conB0, xn_all=xn_all[c],
                         l0_wq=cc(Wq[:, r * 1024:(r + 1) * 1024]),
                         l0_wk=cc(Wk[:, r * 256:(r + 1) * 256]),
                         l0_wv=cc(Wv[:, r * 256:(r + 1) * 256])))
    resB0 = _run(cxB0, maps)
    attn_tok = a2a([resB0[c]["attn_my"] for c in range(NC)])

    cxC0 = _build("C0", build_segment_C, 0)
    maps = []
    for c in range(NC):
        maps.append(dict(conC0, attn_tok=attn_tok[c], x_in=x_fm[c],
                         l0_wo=inputs["L0_attn_Wo"],
                         l0f_cd=cc(inputs["L0_ffn_w_q"].T),
                         l0f_sc=cc(inputs["L0_ffn_scales"].T)))
    resC0 = _run(cxC0, maps)
    x2 = [resC0[c]["x_out"] for c in range(NC)]
    xn2 = [resC0[c]["xn_out"] for c in range(NC)]

    cxB1 = _build("B1", build_segment_B, 1)
    xn_all = gather_xn(xn2)
    qc, kc, vc = (inputs["L1_attn_q_codes"], inputs["L1_attn_k_codes"],
                  inputs["L1_attn_v_codes"])
    qs, ks, vs = (inputs["L1_attn_q_scales"], inputs["L1_attn_k_scales"],
                  inputs["L1_attn_v_scales"])
    maps = []
    for c in range(NC):
        b, r = core_bt(c)
        maps.append(dict(conB1, xn_all=xn_all[c],
                         l1_q_cd=cc(qc[:, r * 1024:(r + 1) * 1024]),
                         l1_k_cd=cc(kc[:, r * 256:(r + 1) * 256]),
                         l1_v_cd=cc(vc[:, r * 256:(r + 1) * 256]),
                         l1_q_sc=cc(qs[:, r * 1024:(r + 1) * 1024]),
                         l1_k_sc=cc(ks[:, r * 256:(r + 1) * 256]),
                         l1_v_sc=cc(vs[:, r * 256:(r + 1) * 256])))
    resB1 = _run(cxB1, maps)
    attn_tok = a2a([resB1[c]["attn_my"] for c in range(NC)])

    cxE = _build("E", build_segment_C, 1)
    A_ = inputs["L1_ffn_A"]
    Bm = inputs["L1_ffn_Bm"]
    gates = cc(np.concatenate([inputs["L1_ffn_Wq"].T, inputs["L1_ffn_Wk"].T,
                               A_.reshape(128, D).T], axis=1))
    BmT = cc(Bm.transpose(0, 2, 1).reshape(128, D))
    b_sb = cc(inputs["L1_ffn_b"].reshape(32, 128).T)
    maps = []
    for c in range(NC):
        maps.append(dict(conC1, attn_tok=attn_tok[c], x_in=x2[c],
                         l1_o_cd=inputs["L1_attn_o_codes"],
                         l1_o_sc=inputs["L1_attn_o_scales"],
                         l1f_WT=cc(np.asarray(inputs["L1_ffn_W"]).T),
                         l1f_b=b_sb, l1f_gates=gates, l1f_BmT=BmT))
    resE = _run(cxE, maps)

    out = np.empty((2, S, D), dtype=np.float32)
    for c in range(NC):
        b, r = core_bt(c)
        out[b, r * TPC:(r + 1) * TPC, :] = resE[c]["x_out"].T
    return out


# version 23
# speedup vs baseline: 1.0251x; 1.0251x over previous
import math

import numpy as np
import ml_dtypes

import concourse.bass as bass
import concourse.mybir as mybir
from concourse.bass_utils import run_bass_kernel_spmd

F32 = mybir.dt.float32
BF16 = mybir.dt.bfloat16
I32 = mybir.dt.int32
AF = mybir.ActivationFunctionType
ALU = mybir.AluOpType
AX = mybir.AxisListType

D = 4096
S = 4096
NC = 8
TPC = 1024
HPC = 8
KVPC = 2
HD = 128
NIT = 32
EXP_OFF = -12.0
INV_SQRT_HD = 1.0 / math.sqrt(HD)

ENGINES = ("sp", "act", "pe", "dve", "pool")



class Stream:
    def __init__(self, name, engine, depth):
        self.name = name
        self.engine = engine
        self.depth = depth
        self.sems = None
        self.uses = [0] * depth
        self.n = 0


class Op:
    __slots__ = ("idx", "engine", "emit", "kind", "reads", "writes", "tag",
                 "deps", "needs_inc", "seq", "event", "waits", "stream",
                 "slot", "val")

    def __init__(self, idx, engine, emit, kind, reads, writes, tag):
        self.idx = idx
        self.engine = engine
        self.emit = emit
        self.kind = kind
        self.reads = tuple(reads)
        self.writes = tuple(writes)
        self.tag = tag
        self.deps = []
        self.needs_inc = False
        self.seq = None
        self.event = None
        self.waits = []
        self.stream = None
        self.slot = None
        self.val = None


def _edge_skippable(p, o):
    return p.kind == "op" and o.kind == "op" and p.engine == "pe" and o.engine == "pe"


class Prog:
    def __init__(self):
        self.ops = []
        self.writers = {}
        self.readers = {}
        self.streams = {}
        self.cc_n = 0

    def stream(self, name, engine, depth=6):
        st = self.streams.get(name)
        if st is None:
            st = self.streams[name] = Stream(name, engine, depth)
        else:
            assert st.engine == engine
        return st

    def _record(self, op):
        akey = (("st", op.stream.name, op.slot) if op.kind == "dma"
                else ("cc",) if op.kind == "cc" else op.engine)
        dep = set()
        for rg in op.reads:
            dep.update(self.writers.get(rg, {}).values())
        for rg in op.writes:
            dep.update(self.writers.get(rg, {}).values())
            dep.update(self.readers.get(rg, {}).values())
        dep.discard(op.idx)
        op.deps = sorted(dep)
        for rg in op.writes:
            self.writers.setdefault(rg, {})[akey] = op.idx
            self.readers[rg] = {}
        for rg in op.reads:
            self.readers.setdefault(rg, {})[akey] = op.idx
        self.ops.append(op)

    def fence(self, keys, tag="fence"):
        self.op("pool", (lambda e: e.memset(self._fence_buf[0:1, 0:1], 0.0)),
                reads=(), writes=tuple(keys), tag=tag)

    def op(self, engine, emit, reads=(), writes=(), tag=""):
        o = Op(len(self.ops), engine, emit, "op", reads, writes, tag)
        self._record(o)
        return o

    def dma(self, stream, emit, reads=(), writes=(), tag=""):
        o = Op(len(self.ops), stream.engine, emit, "dma", reads, writes, tag)
        o.stream = stream
        o.slot = stream.n % stream.depth
        stream.uses[o.slot] += 1
        o.val = stream.uses[o.slot] * 16
        stream.n += 1
        self._record(o)
        return o

    def cc(self, emit, reads=(), writes=(), tag=""):
        o = Op(len(self.ops), "pool", emit, "cc", reads, writes, tag)
        self.cc_n += 1
        o.val = self.cc_n
        self._record(o)
        return o

    def finalize(self, nc, block):
        ops = self.ops
        for o in ops:
            for d in o.deps:
                p = ops[d]
                if p.kind == "op" and not _edge_skippable(p, o):
                    p.needs_inc = True

        eng_sems = {e: nc.semaphore(f"sem_{e}").__enter__() for e in ENGINES}
        cc_sem = nc.semaphore("sem_cc").__enter__() if self.cc_n else None
        for st in self.streams.values():
            st.sems = [nc.semaphore(f"sm_{st.name}{i}").__enter__()
                       for i in range(st.depth)]

        seq = {e: 0 for e in ENGINES}
        for o in ops:
            if o.kind == "op":
                if o.needs_inc:
                    seq[o.engine] += 1
                    o.seq = seq[o.engine]
                    o.event = (("eng", o.engine), o.seq)
            elif o.kind == "dma":
                o.event = (("dma", o.stream.name, o.slot), o.val)
            else:
                o.event = (("cc",), o.val)

        know = {e: {} for e in ENGINES}
        know_at = {}
        for o in ops:
            e = o.engine
            req = []
            for d in o.deps:
                p = ops[d]
                if p.event is None:
                    assert _edge_skippable(p, o), f"no event: {p.tag} <- {o.tag}"
                    continue
                if _edge_skippable(p, o):
                    continue
                req.append((p.event, d))
            if o.kind == "dma" and o.val > 16:
                req.append(((("dma", o.stream.name, o.slot), o.val - 16), None))
            for (sk, v), d in req:
                if know[e].get(sk, 0) >= v:
                    continue
                o.waits.append((sk, v))
                know[e][sk] = v
                if d is not None:
                    for k2, v2 in know_at[d].items():
                        if know[e].get(k2, 0) < v2:
                            know[e][k2] = v2
            snap = dict(know[e])
            if o.event is not None:
                sk, v = o.event
                if snap.get(sk, 0) < v:
                    snap[sk] = v
            know_at[o.idx] = snap

        def sem_of(sk):
            if sk[0] == "eng":
                return eng_sems[sk[1]]
            if sk[0] == "dma":
                return self.streams[sk[1]].sems[sk[2]]
            return cc_sem

        drain = []
        for st in self.streams.values():
            for slot in range(st.depth):
                if st.uses[slot]:
                    drain.append((("dma", st.name, slot), st.uses[slot] * 16))
        if cc_sem is not None:
            drain.append((("cc",), self.cc_n))
        drain = [(sk, v) for sk, v in drain if know["sp"].get(sk, 0) < v]

        eng_map = {"sp": block.sync, "act": block.scalar, "pe": block.tensor,
                   "dve": block.vector, "pool": block.gpsimd}
        by_eng = {e: [] for e in ENGINES}
        for o in ops:
            by_eng[o.engine].append(o)

        def emit_for(eng_name):
            def run(eng):
                for o in by_eng[eng_name]:
                    for sk, v in o.waits:
                        eng.wait_ge(sem_of(sk), v)
                    inst = o.emit(eng)
                    if o.kind == "dma":
                        inst.then_inc(o.stream.sems[o.slot], 16)
                    elif o.kind == "cc":
                        inst.then_inc(cc_sem)
                    elif o.needs_inc:
                        inst.then_inc(eng_sems[o.engine], 1)
                if eng_name == "sp":
                    for sk, v in drain:
                        eng.wait_ge(sem_of(sk), v)
            return run

        for name in ENGINES:
            if by_eng[name] or name == "sp":
                eng_map[name](emit_for(name))



def make_consts():
    bf = ml_dtypes.bfloat16
    inv_freq = 1.0 / (10000.0 ** (np.arange(64, dtype=np.float64) / 64.0))
    ang = np.arange(S, dtype=np.float64)[None, :] * inv_freq[:, None]
    cosd = np.concatenate([np.cos(ang), np.cos(ang)], axis=0)
    sinds = np.concatenate([-np.sin(ang), np.sin(ang)], axis=0)
    cosd = cosd.astype(np.float32)
    sinds = sinds.astype(np.float32)
    kk = np.arange(128)[:, None]
    qq = np.arange(1024)[None, :]
    masks = np.concatenate(
        [((u * 128 + kk) <= qq)[:, (0 if u < 4 else 512):
                                (512 if u < 4 else 1024)].astype(np.float32)
         for u in range(8)],
        axis=1).astype(bf)
    selblk = np.zeros((128, 4096), dtype=np.float32)
    p_ = np.arange(128)
    for t in range(32):
        selblk[4 * t + p_ // 32, t * 128 + p_] = 1.0
    selblk = selblk.astype(bf)
    sel32h = np.zeros((32, 4096), dtype=np.float32)
    for g in range(32):
        sel32h[g, g * 128:(g + 1) * 128] = 0.5
    sel32h = sel32h.astype(bf)
    identf = np.eye(128, dtype=np.float32)
    onesb = np.ones((1, 128), dtype=bf)
    onesc = np.ones((128, 1), dtype=bf)
    cvec = np.zeros((128, 8), dtype=np.float32)
    cvec[:, 0] = INV_SQRT_HD
    cvec[:, 1] = EXP_OFF
    cvec[:, 2] = 1.0 / D
    cvec[:, 3] = 1e-5
    cvec[:, 4] = 1.0 / math.sqrt(32.0)
    cvec[:, 6] = -1.0
    return dict(c_cosd=cosd, c_sinds=sinds, c_masks=masks, c_selblk=selblk,
                c_sel32h=sel32h, c_identf=identf, c_onesb=onesb,
                c_onesc=onesc, c_cvec=cvec)


CONST_DECLS = [
    ("c_cosd", [128, S], F32), ("c_sinds", [128, S], F32),
    ("c_masks", [128, 4096], BF16),
    ("c_selblk", [128, 4096], BF16), ("c_sel32h", [32, 4096], BF16),
    ("c_identf", [128, 128], F32), ("c_onesb", [1, 128], BF16),
    ("c_onesc", [128, 1], BF16), ("c_cvec", [128, 8], F32),
]



class Ctx:
    pass


def new_ctx():
    cx = Ctx()
    cx.nc = bass.Bass("TRN2", target_bir_lowering=False)
    cx.pg = Prog()
    cx.pg._fence_buf = cx.nc.alloc_sbuf_tensor("fence_buf", [1, 4], F32)
    cx.sb = {}
    cx.ps = {}
    cx.dr = {}
    return cx


def sb_alloc(cx, name, shape, dtype):
    h = cx.nc.alloc_sbuf_tensor(name, shape, dtype)
    cx.sb[name] = h
    return h


def dram(cx, name, shape, dtype, kind="Internal"):
    h = cx.nc.dram_tensor(name, shape, dtype, kind=kind)
    cx.dr[name] = h
    return h


class HalfBank:

    def __init__(self, t, off):
        self.t = t
        self.off = off

    def __getitem__(self, idx):
        if not isinstance(idx, tuple):
            rows, cols = idx, slice(None)
        else:
            rows, cols = idx
        if cols == slice(None):
            cols = slice(0, 512)
        start = (cols.start or 0) + self.off
        stop = (512 if cols.stop is None else cols.stop) + self.off
        return self.t[rows, start:stop]


def decl_psum(cx):
    cx.pp = [cx.nc.alloc_psum_tensor(f"pp{i}", [128, 1024], F32)
             for i in range(4)]
    for i in range(8):
        cx.ps[i] = HalfBank(cx.pp[i // 2], (i % 2) * 512)


def decl_common(cx, const_names=None):
    ld = cx.pg.stream("cload", "sp", depth=4)
    ldc = cx.pg.stream("cloadc", "pool", depth=2)
    for name, shape, dt in CONST_DECLS:
        if const_names is not None and name not in const_names:
            continue
        dram(cx, name, shape, dt, kind="ExternalInput")
        if name in ("c_masks", "c_selblk"):
            continue
        if name in ("c_cosd", "c_sinds"):
            s = sb_alloc(cx, name[2:], shape, BF16)
            d = cx.dr[name]
            cx.pg.dma(ldc, (lambda e, a=s, b=d: e.dma_start(out=a[:], in_=b.ap())),
                      reads=[name], writes=[name[2:]], tag=f"ld_{name}")
            continue
        s = sb_alloc(cx, name[2:], shape, dt)
        d = cx.dr[name]
        cx.pg.dma(ld, (lambda e, a=s, b=d: e.dma_start(out=a[:], in_=b.ap())),
                  reads=[name], writes=[name[2:]], tag=f"ld_{name}")
    decl_psum(cx)


def cv(cx, j, p=128):
    return cx.sb["cvec"][0:p, j:j + 1]



def emit_ln_stats_reset(cx, tag):
    return {
        "pxs": [(tag, "pxs", tb) for tb in range(2)],
        "pss": [(tag, "pss", tb) for tb in range(2)],
    }


def emit_ln_tile_stats(cx, tag, it, rxb, xbv, rsq_src, sq_t, first, last):
    pg = cx.pg
    onesc = cx.sb["onesc"]
    for tb in range(2):
        pg.op("pe", (lambda e, b=xbv, t=tb, s=first, z=last:
                     e.matmul(cx.pp[2][0:1, t * 512:(t + 1) * 512],
                              onesc[:, 0:1], b[:, t * 512:(t + 1) * 512],
                              start=s, stop=z)),
              reads=[rxb, "onesc"], writes=[("PB", 4 + tb)],
              tag=f"{tag}mmx{it}_{tb}")
        pg.op("pe", (lambda e, b=sq_t, t=tb, s=first, z=last:
                     e.matmul(cx.pp[3][0:1, t * 512:(t + 1) * 512],
                              onesc[:, 0:1], b[:, t * 512:(t + 1) * 512],
                              start=s, stop=z)),
              reads=[rsq_src, "onesc"], writes=[("PB", 6 + tb)],
              tag=f"{tag}mms{it}_{tb}")


def emit_ln_finalize(cx, tag):
    pg = cx.pg
    stats = cx.sb["stats"]
    srow = cx.sb["srow"]
    bct = cx.sb["bct"]
    onesb = cx.sb["onesb"]
    pxs = [cx.ps[4], cx.ps[5]]
    pss = [cx.ps[6], cx.ps[7]]
    pbc = cx.ps[3]
    rstats = ("LN", "stats")
    rsrow = ("LN", "srow")
    rbct = ("LN", "bct")
    for tb in range(2):
        mv = stats[0:1, tb * 512:(tb + 1) * 512]
        vv = stats[0:1, 1024 + tb * 512:1024 + (tb + 1) * 512]
        sc = stats[0:1, 2048 + tb * 512:2048 + (tb + 1) * 512]
        pg.op("dve", (lambda e, a=mv, p=pxs[tb]:
                      e.tensor_scalar(a, p[0:1, :], 1.0 / D, None, ALU.mult)),
              reads=[("PB", 4 + tb)], writes=[rstats], tag=f"{tag}m{tb}")
        pg.op("dve", (lambda e, a=vv, p=pss[tb]:
                      e.tensor_scalar(a, p[0:1, :], 1.0 / D, None, ALU.mult)),
              reads=[("PB", 6 + tb)], writes=[rstats], tag=f"{tag}v{tb}")
        pg.op("dve", (lambda e, a=sc, m=mv: e.tensor_mul(a, m, m)),
              reads=[rstats], writes=[rstats], tag=f"{tag}m2{tb}")
        pg.op("dve", (lambda e, a=vv, b=sc: e.tensor_sub(a, a, b)),
              reads=[rstats], writes=[rstats], tag=f"{tag}var{tb}")
        pg.op("act", (lambda e, a=vv: e.activation(a, a, AF.Sqrt, bias=cv(cx, 3, 1))),
              reads=[rstats, "cvec"], writes=[rstats], tag=f"{tag}sd{tb}")
        pg.op("dve", (lambda e, a=vv: e.reciprocal(a, a)),
              reads=[rstats], writes=[rstats], tag=f"{tag}rs{tb}")
        pg.op("dve", (lambda e, a=sc, m=mv, s=vv: e.tensor_mul(a, m, s)),
              reads=[rstats], writes=[rstats], tag=f"{tag}ms{tb}")
        pg.op("dve", (lambda e, a=sc: e.tensor_scalar(a, a, -1.0, None, ALU.mult)),
              reads=[rstats], writes=[rstats], tag=f"{tag}nms{tb}")
    pg.op("dve", (lambda e: e.tensor_copy(srow[0:1, 0:1024], stats[0:1, 1024:2048])),
          reads=[rstats], writes=[rsrow], tag=f"{tag}cr1")
    pg.op("dve", (lambda e: e.tensor_copy(srow[0:1, 1024:2048], stats[0:1, 2048:3072])),
          reads=[rstats], writes=[rsrow], tag=f"{tag}cr2")
    for j in range(4):
        src = srow[0:1, j * 512:(j + 1) * 512]
        dst = bct[:, j * 512:(j + 1) * 512]
        pg.op("pe", (lambda e, s_=src: e.matmul(pbc[:, 0:512], onesb[0:1, :], s_,
                                                start=True, stop=True)),
              reads=[rsrow, "onesb"], writes=[("PB", 3)], tag=f"{tag}bc{j}")
        pg.op("dve", (lambda e, d_=dst: e.tensor_copy(d_, pbc[:, 0:512])),
              reads=[("PB", 3)], writes=[rbct], tag=f"{tag}bcc{j}")
    return rbct


def emit_ln_apply(cx, tag, it, xbv, rxb, rbct, xn_dst_name, xn_dst, st_out):
    pg = cx.pg
    bct = cx.sb["bct"]
    pg.op("dve", (lambda e, a=xbv: e.tensor_mul(a, a, bct[:, 0:1024])),
          reads=[rxb, rbct], writes=[rxb], tag=f"{tag}ap1_{it}")
    pg.op("dve", (lambda e, a=xbv: e.tensor_add(a, a, bct[:, 1024:2048])),
          reads=[rxb, rbct], writes=[rxb], tag=f"{tag}ap2_{it}")
    if xn_dst is not None:
        pg.dma(st_out, (lambda e, a=xbv, i=it:
                        e.dma_start(out=xn_dst.ap()[i * 128:(i + 1) * 128, :], in_=a)),
               reads=[rxb], writes=[("XD", xn_dst_name, it, 0),
                                    ("XD", xn_dst_name, it, 1)],
               tag=f"{tag}sxn{it}")


def emit_ln_standalone(cx, x_src_name, xn_dst_name, tag):
    pg = cx.pg
    x_src = cx.dr[x_src_name]
    xn_dst = cx.dr[xn_dst_name] if xn_dst_name else None
    ld = pg.stream("xload", "sp", depth=6)
    st_out = pg.stream("agout", "act", depth=6)
    xb = cx.sb["xb_slab"]
    xts = [cx.sb["xtile0"], cx.sb["xtile1"]]
    sqs = [cx.sb["sq0"], cx.sb["sq1"]]
    for it in range(NIT):
        xt = xts[it % 2]
        sq_t = sqs[it % 2]
        rx = ("LN", "xt", it % 2)
        rsq = ("LN", "sq", it % 2)
        rxb = (tag, "xb", it)
        pg.dma(ld, (lambda e, a=xt, b=x_src, i=it:
                    e.dma_start(out=a[:], in_=b.ap()[i * 128:(i + 1) * 128, :])),
               reads=[x_src_name], writes=[rx], tag=f"{tag}ldx{it}")
        xbv = xb[:, it * 1024:(it + 1) * 1024]
        pg.op("dve", (lambda e, a=xbv, b=xt: e.tensor_copy(a, b[:])),
              reads=[rx], writes=[rxb], tag=f"{tag}cast{it}")
        pg.op("act", (lambda e, a=sq_t, b=xt: e.activation(a[:], b[:], AF.Square)),
              reads=[rx], writes=[rsq], tag=f"{tag}sq{it}")
        emit_ln_tile_stats(cx, tag, it, rxb, xbv, rsq, sq_t[:],
                           first=(it == 0), last=(it == NIT - 1))
    rbct = emit_ln_finalize(cx, tag)
    for it in range(NIT):
        xbv = xb[:, it * 1024:(it + 1) * 1024]
        emit_ln_apply(cx, tag, it, xbv, (tag, "xb", it), rbct,
                      xn_dst_name, xn_dst, st_out)



def build_segment_A():
    cx = new_ctx()
    nc = cx.nc
    decl_common(cx, const_names={"c_onesb", "c_onesc", "c_cvec"})
    dram(cx, "x_fm", [D, TPC], F32, kind="ExternalInput")
    dram(cx, "xn0", [D, TPC], BF16, kind="ExternalOutput")
    sb_alloc(cx, "xb_slab", [128, NIT * 1024], BF16)
    sb_alloc(cx, "xtile0", [128, 1024], F32)
    sb_alloc(cx, "xtile1", [128, 1024], F32)
    sb_alloc(cx, "sq0", [128, 1024], BF16)
    sb_alloc(cx, "sq1", [128, 1024], BF16)
    sb_alloc(cx, "stats", [1, 3072], F32)
    sb_alloc(cx, "srow", [1, 2048], BF16)
    sb_alloc(cx, "bct", [128, 2048], BF16)
    emit_ln_standalone(cx, "x_fm", "xn0", "lnA")
    with nc.Block() as block:
        cx.pg.finalize(nc, block)
    nc.finalize()
    return cx





def emit_dequant_qkv_to_dram(cx, tag):
    pg = cx.pg
    sc = cx.sb["scalesT"]
    scf = cx.sb["scalesF"]
    sel = cx.sb["sel32h"]
    ldsc = pg.stream("scload", "pool", depth=2)
    ldcd = pg.stream("cdload", "sp", depth=4)
    stw = pg.stream("wdq", "act", depth=6)
    for nm, off, w in [("l1_q_sc", 0, 1024), ("l1_k_sc", 1024, 256),
                       ("l1_v_sc", 1280, 256)]:
        d = cx.dr[nm]
        pg.dma(ldsc, (lambda e, d_=d, o=off, w_=w:
                      e.dma_start(out=sc[0:32, o:o + w_], in_=d_.ap())),
               reads=[nm], writes=[(tag, "sc")], tag=f"{tag}ld{nm}")
        pg.dma(ldsc, (lambda e, d_=d, o=off, w_=w:
                      e.dma_start(out=scf[0:32, o:o + w_], in_=d_.ap())),
               reads=[nm], writes=[(tag, "scf")], tag=f"{tag}ldf{nm}")
    pg.op("dve", (lambda e: e.tensor_sub(scf[0:32, :], scf[0:32, :],
                                         sc[0:32, 0:1536])),
          reads=[(tag, "sc"), (tag, "scf")], writes=[(tag, "scf")], tag=f"{tag}lo1")
    pg.op("dve", (lambda e: e.tensor_copy(sc[0:32, 1536:3072], scf[0:32, :])),
          reads=[(tag, "scf")], writes=[(tag, "sc")], tag=f"{tag}lo2")
    cdb = cx.sb["cd_buf0"]
    wdq = [cx.sb[f"wdq{j}"] for j in range(3)]
    ps_scs = [cx.ps[0], cx.ps[1]]
    blocks = [("l1_q_cd", 0, 512, 0, "wq_deq", 0),
              ("l1_q_cd", 512, 512, 512, "wq_deq", 512),
              ("l1_k_cd", 0, 256, 1024, "wk_deq", 0),
              ("l1_v_cd", 0, 256, 1280, "wv_deq", 0)]
    nb_tot = 0
    for it in range(NIT):
        for nm, scol, w, sccol, dnm, dcol in blocks:
            half = nb_tot % 2
            cbv = cdb[:, half * 512:half * 512 + w]
            rcb = (tag, "cd", half)
            d = cx.dr[nm]
            pg.dma(ldcd, (lambda e, d_=d, a=cbv, i=it, sc0=scol, w_=w:
                          e.dma_start(out=a,
                                      in_=d_.ap()[i * 128:(i + 1) * 128,
                                                  sc0:sc0 + w_])),
                   reads=[nm], writes=[rcb], tag=f"{tag}cd{it}_{nm}{scol}")
            wb = wdq[nb_tot % 3]
            rwb = (tag, "wdq", nb_tot % 3)
            nb_tot += 1
            ps_sc = ps_scs[nb_tot % 2]
            pbk = ("PB", 0 + nb_tot % 2)
            pg.op("pe", (lambda e, i=it, o=sccol, w_=w, p=ps_sc:
                         e.matmul(p[:, 0:w_], sel[:, i * 128:(i + 1) * 128],
                                  sc[0:32, o:o + w_], start=True, stop=False)),
                  reads=[(tag, "sc"), "sel32h"], writes=[pbk],
                  tag=f"{tag}scmm{it}_{sccol}")
            pg.op("pe", (lambda e, i=it, o=sccol + 1536, w_=w, p=ps_sc:
                         e.matmul(p[:, 0:w_], sel[:, i * 128:(i + 1) * 128],
                                  sc[0:32, o:o + w_], start=False, stop=True)),
                  reads=[(tag, "sc"), "sel32h"], writes=[pbk],
                  tag=f"{tag}scmm2{it}_{sccol}")
            pg.op("dve", (lambda e, b=cbv, w_=w, a=wb, p=ps_sc:
                          e.scalar_tensor_tensor(a[:, 0:w_], b, -8.0,
                                                 p[:, 0:w_], ALU.add, ALU.mult)),
                  reads=[rcb, pbk], writes=[rwb], tag=f"{tag}dq{it}_{sccol}")
            dd = cx.dr[dnm]
            pg.dma(stw, (lambda e, a=wb, d_=dd, i=it, c=dcol, w_=w:
                         e.dma_start(out=d_.ap()[i * 128:(i + 1) * 128, c:c + w_],
                                     in_=a[:, 0:w_])),
                   reads=[rwb], writes=[dnm], tag=f"{tag}wst{it}_{sccol}")


QKV_UNITS = ([("q", i) for i in range(8)] + [("k", 0), ("k", 1),
             ("v", 0), ("v", 1)])


def emit_qkv(cx, l, tag):
    pg = cx.pg
    xb = cx.sb["xb_slab"]
    wu = [cx.sb[f"wu{j}"] for j in range(3)]
    cosd = cx.sb["cosd"]
    sinds = cx.sb["sinds"]
    qsets = [(cx.sb["qstage0"], cx.sb["qswap0"], cx.sb["qrot0"], cx.sb["utmp0"]),
             (cx.sb["qstage1"], cx.sb["qswap1"], cx.sb["qrot1"], cx.sb["utmp1"])]
    vst = cx.sb["vstage"]
    vbf = cx.sb["vbf"]
    identf = cx.sb["identf"]
    ldxn = pg.stream("xnload", "sp", depth=3)
    swp = pg.stream("swp", "act", depth=4)
    ldw = pg.stream(f"wload{l}", "pool" if l == 0 else "sp", depth=3)
    stq = pg.stream("qout", "act", depth=6)
    xn_all = cx.dr["xn_all"]
    q_scr = cx.dr["q_scr"]
    k_scr = cx.dr["k_scr"]
    v_scr = cx.dr["v_scr"]
    if l == 0:
        wsrc = {"q": cx.dr["l0_wq"], "k": cx.dr["l0_wk"], "v": cx.dr["l0_wv"]}
    else:
        wsrc = {"q": cx.dr["wq_deq"], "k": cx.dr["wk_deq"], "v": cx.dr["wv_deq"]}
    banks = [cx.ps[j] for j in range(8)]
    bank_ctr = 0
    for rb in range(4):
        rxn = (tag, "xn")
        for f in range(4):
            pg.dma(ldxn, (lambda e, r=rb, f_=f: e.dma_start(
                out=xb[:, f_ * 8192:(f_ + 1) * 8192].rearrange(
                    "p (a c) -> p a c", c=1024),
                in_=xn_all.ap()[f_ * D + r * 1024:f_ * D + (r + 1) * 1024,
                                :].rearrange("(a p) c -> p a c", p=128))),
                reads=[("xn_all", f)], writes=[rxn], tag=f"{tag}xn{rb}_{f}")
        for ui, (kind, idx) in enumerate(QKV_UNITS):
            wb = wu[(rb * 12 + ui) % 3]
            rw = (tag, "wu", (rb * 12 + ui) % 3)
            src_d = wsrc[kind]
            col0 = idx * 128
            pg.dma(ldw, (lambda e, s=src_d, c0=col0, w_=wb:
                         e.dma_start(out=w_[:].rearrange("p (a c) -> p a c", c=128),
                                     in_=s.ap()[:, c0:c0 + 128].rearrange(
                                         "(a p) c -> p a c", p=128))),
                   reads=[src_d.name], writes=[rw], tag=f"{tag}w{rb}_{ui}")
            b0 = bank_ctr % 8
            bank_ctr += 2
            pbs = [banks[b0], banks[(b0 + 1) % 8]]
            pair = cx.pp[b0 // 2]
            for it in range(NIT):
                first, last = it == 0, it == NIT - 1
                for tb in range(2):
                    pg.op("pe", (lambda e, p=pair, w_=wb, i=it, t_=tb,
                                 s=first, z=last:
                                 e.matmul(p[:, t_ * 512:(t_ + 1) * 512],
                                          w_[:, i * 128:(i + 1) * 128],
                                          xb[:, i * 1024 + t_ * 512:
                                             i * 1024 + (t_ + 1) * 512],
                                          start=s, stop=z)),
                          reads=[rw, rxn], writes=[("PB", b0 + tb)],
                          tag=f"{tag}mm{rb}_{ui}_{it}{tb}")
            for tb in range(2):
                pb = pbs[tb]
                rpb = ("PB", (b0 + tb) % 8)
                tcol = rb * 1024 + tb * 512
                if kind in ("q", "k"):
                    si = (rb * 24 + ui * 2 + tb) % 2
                    qst, qsw, qrot, ut = qsets[si]
                    rqs = (tag, "qst", si)
                    rqw = (tag, "qsw", si)
                    rqr = (tag, "qrot", si)
                    rut = (tag, "ut", si)
                    pg.op("act", (lambda e, p=pb, a=qst: e.copy(a[:], p[:])),
                          reads=[rpb], writes=[rqs], tag=f"{tag}ev{rb}_{ui}{tb}")
                    pg.dma(swp, (lambda e, a=qst, b=qsw: e.dma_start(
                        out=b[0:64, :], in_=a[64:128, :])),
                        reads=[rqs], writes=[rqw], tag=f"{tag}sw1_{rb}{ui}{tb}")
                    pg.dma(swp, (lambda e, a=qst, b=qsw: e.dma_start(
                        out=b[64:128, :], in_=a[0:64, :])),
                        reads=[rqs], writes=[rqw], tag=f"{tag}sw2_{rb}{ui}{tb}")
                    c_sl = cosd[:, tcol:tcol + 512]
                    s_sl = sinds[:, tcol:tcol + 512]
                    pg.op("dve", (lambda e, a=qst, u=ut, c=c_sl:
                                  e.tensor_mul(u[:, 0:512], a[:], c)),
                          reads=[rqs, "cosd"], writes=[rut], tag=f"{tag}r1_{rb}{ui}{tb}")
                    pg.op("dve", (lambda e, b=qsw, u=ut, s=s_sl:
                                  e.tensor_mul(u[:, 512:1024], b[:], s)),
                          reads=[rqw, "sinds"], writes=[rut], tag=f"{tag}r2_{rb}{ui}{tb}")
                    pg.op("dve", (lambda e, u=ut, o=qrot:
                                  e.tensor_add(o[:], u[:, 0:512], u[:, 512:1024])),
                          reads=[rut], writes=[rqr], tag=f"{tag}r3_{rb}{ui}{tb}")
                    dst = q_scr if kind == "q" else k_scr
                    dkey = ("q_scr", idx) if kind == "q" else ("k_scr", idx)
                    row0 = idx * 128
                    pg.dma(stq, (lambda e, o=qrot, d_=dst, r0=row0, c0=tcol:
                                 e.dma_start(out=d_.ap()[r0:r0 + 128, c0:c0 + 512],
                                             in_=o[:])),
                           reads=[rqr], writes=[dkey], tag=f"{tag}st{rb}_{ui}{tb}")
                else:
                    rvs = (tag, "vst")
                    rvb = (tag, "vbf")
                    pg.op("dve", (lambda e, p=pb: e.tensor_copy(vst[:], p[:])),
                          reads=[rpb], writes=[rvs], tag=f"{tag}vev{rb}{tb}")
                    for sub in range(4):
                        pg.op("pe", (lambda e, p=pb, s_=sub:
                                     e.transpose(p[:, 0:128],
                                                 vst[:, s_ * 128:(s_ + 1) * 128],
                                                 identf[:])),
                              reads=[rvs, "identf"], writes=[rpb],
                              tag=f"{tag}vtr{rb}{tb}{sub}")
                        pg.op("dve", (lambda e, p=pb: e.tensor_copy(vbf[:, 0:128],
                                                                    p[:, 0:128])),
                              reads=[rpb], writes=[rvb], tag=f"{tag}vcp{rb}{tb}{sub}")
                        trow = tcol + sub * 128
                        pg.dma(stq, (lambda e, r0=trow, c0=idx * 128:
                                     e.dma_start(
                                         out=v_scr.ap()[r0:r0 + 128, c0:c0 + 128],
                                         in_=vbf[:, 0:128])),
                               reads=[rvb], writes=[("v_scr", idx)],
                               tag=f"{tag}vst{rb}{tb}{sub}")


def emit_attn(cx, tag, dst_name="attn_my", chunk_cc=None, work_cb=None):
    pg = cx.pg
    q_scr, k_scr, v_scr = cx.dr["q_scr"], cx.dr["k_scr"], cx.dr["v_scr"]
    attn_my = cx.dr[dst_name]
    mld = pg.stream("mld", "pool", depth=1)
    mk_d = cx.dr["c_masks"]
    pg.dma(mld, (lambda e: e.dma_start(out=cx.sb["masks"][:], in_=mk_d.ap())),
           reads=["c_masks"], writes=["masks"], tag=f"{tag}mld")
    qb_bufs = [cx.sb["q_buf0"], cx.sb["q_buf1"]]
    k_bufs = [cx.sb["k_buf0"], cx.sb["k_buf1"]]
    vb = cx.sb["v_buf0"]
    p_bufs = [cx.sb[f"p_buf{j}"] for j in range(4)]
    lnl = cx.sb["lnl2"]
    rinv = cx.sb["rinv_sb"]
    lsel = cx.sb["lsel"]
    onesbf = cx.sb["onesbf"]
    ast = [cx.sb["a_st0"], cx.sb["a_st1"]]
    masks = cx.sb["masks"]
    onessq = cx.sb["onessq"]
    ldq = pg.stream("qld", "sp", depth=2)
    ldkv = pg.stream("kvld", "sp", depth=2)
    sta = pg.stream("aout", "act", depth=4)
    pp = cx.pp
    rv = (tag, "vbuf")
    rr = (tag, "rinv")
    for qg in range(4):
        nkt = 8 * (qg + 1)
        for h in range(HPC):
            qb_ = qb_bufs[h % 2]
            rq = (tag, "q", h % 2)
            pg.dma(ldq, (lambda e, b=qb_, h_=h, g=qg: e.dma_start(
                out=b[:], in_=q_scr.ap()[h_ * 128:(h_ + 1) * 128,
                                         g * 1024:(g + 1) * 1024])),
                reads=[("q_scr", h)], writes=[rq], tag=f"{tag}ldq{qg}_{h}")
            kvl = h // 4
            kb = k_bufs[kvl]
            rk = (tag, "kbuf", kvl)
            if h % 4 == 0:
                pg.dma(ldkv, (lambda e, b=kb, kv=kvl, n=nkt: e.dma_start(
                    out=b[:, 0:n * 128],
                    in_=k_scr.ap()[kv * 128:(kv + 1) * 128, 0:n * 128])),
                    reads=[("k_scr", kvl)], writes=[rk], tag=f"{tag}ldk{qg}_{kvl}")
                pg.dma(ldkv, (lambda e, kv=kvl, n=nkt: e.dma_start(
                    out=vb[:, 0:n * 128].rearrange("p (a c) -> p a c", c=128),
                    in_=v_scr.ap()[0:n * 128, kv * 128:(kv + 1) * 128].rearrange(
                        "(a p) c -> p a c", p=128))),
                    reads=[("v_scr", kvl)], writes=[rv], tag=f"{tag}ldv{qg}_{kvl}")
            nkt0 = 8 * qg + 4

            def emit_pv(kt, nkt=nkt, nkt0=nkt0, rv=rv):
                pb = p_bufs[kt % 4]
                rp = (tag, "p", kt % 4)
                onesc = cx.sb["onesc"]
                for hf in range(2):
                    if hf == 0 and kt >= nkt0:
                        continue
                    z = (kt == nkt0 - 1) if hf == 0 else (kt == nkt - 1)
                    pg.op("pe", (lambda e, o=pb, c0=kt * 128, s=(kt == 0),
                                 z=z, q0=hf * 512:
                                 e.matmul(pp[2][:, q0:q0 + 512],
                                          vb[:, c0:c0 + 128],
                                          o[:, q0:q0 + 512], start=s, stop=z)),
                          reads=[rv, rp], writes=[("PB", 4 + hf)],
                          tag=f"{tag}pv{qg}_{h}_{kt}_{hf}")
                if kt < nkt0:
                    pg.op("pe", (lambda e, o=pb, s=(kt == 0),
                                 z=(kt == nkt0 - 1):
                                 e.matmul(cx.ps[6][0:1, 0:512], onesc[:, 0:1],
                                          o[:, 0:512], start=s, stop=z)),
                          reads=[rp, "onesc"], writes=[("PB", 6)],
                          tag=f"{tag}la{qg}_{h}_{kt}")
                pg.op("pe", (lambda e, o=pb, s=(kt == 0), z=(kt == nkt - 1):
                             e.matmul(cx.ps[6][32:33, 0:512], onesc[:, 0:1],
                                      o[:, 512:1024], start=s, stop=z)),
                      reads=[rp, "onesc"], writes=[("PB", 6)],
                      tag=f"{tag}lc{qg}_{h}_{kt}")

            for kt in range(nkt):
                live0 = kt < nkt0
                sp_ = pp[kt % 2]
                rs = [("PB", (kt % 2) * 2), ("PB", (kt % 2) * 2 + 1)]
                for hf in range(2):
                    if hf == 0 and not live0:
                        continue
                    pg.op("pe", (lambda e, p=sp_, c0=kt * 128, q_=qb_,
                                 k_=kb, o=hf * 512:
                                 e.matmul(p[:, o:o + 512], k_[:, c0:c0 + 128],
                                          q_[:, o:o + 512],
                                          start=True, stop=True)),
                          reads=[rk, rq], writes=[rs[hf]],
                          tag=f"{tag}s{qg}_{h}_{kt}_{hf}")
                pb = p_bufs[kt % 4]
                rp = (tag, "p", kt % 4)
                if live0:
                    pg.op("act", (lambda e, p=sp_, o=pb:
                                  e.activation(o[:], p[:], AF.Exp, bias=cv(cx, 1),
                                               scale=cv(cx, 0))),
                          reads=rs + ["cvec"], writes=[rp],
                          tag=f"{tag}e{qg}_{h}_{kt}")
                else:
                    pg.op("act", (lambda e, p=sp_, o=pb:
                                  e.activation(o[:, 512:1024], p[:, 512:1024],
                                               AF.Exp, bias=cv(cx, 1),
                                               scale=cv(cx, 0))),
                          reads=[rs[1], "cvec"], writes=[rp],
                          tag=f"{tag}e{qg}_{h}_{kt}")
                if kt >= 8 * qg:
                    j = kt - 8 * qg
                    c0m = 0 if j < 4 else 512
                    pg.op("dve", (lambda e, o=pb, j_=j, c_=c0m:
                                  e.tensor_mul(o[:, c_:c_ + 512],
                                               o[:, c_:c_ + 512],
                                               masks[:, j_ * 512:(j_ + 1) * 512])),
                          reads=[rp, "masks"], writes=[rp],
                          tag=f"{tag}m{qg}_{h}_{kt}")
                if kt >= 1:
                    emit_pv(kt - 1)
                if work_cb is not None:
                    work_cb()
            emit_pv(nkt - 1)
            pg.op("act", (lambda e: e.activation(lnl[:], cx.ps[6][0:33, 0:512],
                                                 AF.Ln)),
                  reads=[("PB", 6)], writes=[(tag, "lnl")],
                  tag=f"{tag}rl{qg}_{h}")
            for hf in range(2):
                r0l = hf * 32
                pg.op("pe", (lambda e, o=hf * 512, f=r0l:
                             e.matmul(pp[1][:, o:o + 512], onesbf[f:f + 1, :],
                                      lnl[f:f + 1, 0:512],
                                      start=True, stop=True)),
                      reads=[(tag, "lnl"), "onesbf"], writes=[("PB", 2 + hf)],
                      tag=f"{tag}bc{qg}_{h}_{hf}")
            pg.op("act", (lambda e: e.activation(rinv[:], pp[1][:], AF.Exp,
                                                 scale=cv(cx, 6))),
                  reads=[("PB", 2), ("PB", 3), "cvec"], writes=[rr],
                  tag=f"{tag}rc{qg}_{h}")
            av = ast[h % 2]
            ra = (tag, "ast", h % 2)
            pg.op("dve", (lambda e, a=av: e.tensor_mul(a[:], pp[2][:], rinv[:])),
                  reads=[("PB", 4), ("PB", 5), rr], writes=[ra],
                  tag=f"{tag}nm{qg}_{h}")
            r0 = qg * 1024 + h * 128
            pg.dma(sta, (lambda e, a=av, r_=r0:
                         e.dma_start(out=attn_my.ap()[r_:r_ + 128, :], in_=a[:])),
                   reads=[ra], writes=[("a2a", qg)], tag=f"{tag}ao{qg}_{h}")
        if chunk_cc is not None:
            chunk_cc(qg)


ATTN_BUF_SPECS = [
    ("q_buf0", [128, 1024], BF16, 2048), ("q_buf1", [128, 1024], BF16, 2048),
    ("k_buf0", [128, 4096], BF16, 8192), ("k_buf1", [128, 4096], BF16, 8192),
    ("v_buf0", [128, 4096], BF16, 8192),
    ("p_buf0", [128, 1024], BF16, 2048), ("p_buf1", [128, 1024], BF16, 2048),
    ("p_buf2", [128, 1024], BF16, 2048), ("p_buf3", [128, 1024], BF16, 2048),
    ("lnl2", [33, 512], F32, 2048),
    ("rinv_sb", [128, 1024], F32, 4096),
    ("a_st0", [128, 1024], BF16, 2048), ("a_st1", [128, 1024], BF16, 2048),
]
DQ_BUF_SPECS = [
    ("scalesT", [32, 3072], BF16, 6144), ("scalesF", [32, 1536], F32, 6144),
    ("cd_buf0", [128, 1024], I32, 4096),
    ("wdq0", [128, 512], BF16, 1024), ("wdq1", [128, 512], BF16, 1024),
    ("wdq2", [128, 512], BF16, 1024),
]


def decl_segB(cx, l):
    sb_alloc(cx, "masks", [128, 4096], BF16)
    sb_alloc(cx, "onessq", [128, 128], F32)
    sb_alloc(cx, "lsel", [128, 4], BF16)
    sb_alloc(cx, "onesbf", [33, 128], F32)
    sb_alloc(cx, "xb_slab", [128, NIT * 1024], BF16)
    sb_alloc(cx, "wgrp0", [128, NIT * 512], BF16)
    sb_alloc(cx, "wgrp1", [128, NIT * 512], BF16)
    sb_alloc(cx, "qstage", [128, 512], F32)
    sb_alloc(cx, "qswap", [128, 512], F32)
    sb_alloc(cx, "qrot", [128, 512], BF16)
    sb_alloc(cx, "utmp", [128, 1024], F32)
    sb_alloc(cx, "vstage", [128, 512], F32)
    sb_alloc(cx, "vbf", [128, 512], BF16)
    sb_alloc(cx, "onescf", [128, 1], F32)
    sb_alloc(cx, "onesrf", [1, 128], F32)
    base = cx.nc.lookup_mloc(cx.sb["wgrp0"]).addr
    off = base
    for nm, shape, dt, sz in ATTN_BUF_SPECS:
        cx.sb[nm] = cx.nc.alloc_sbuf_tensor_at(nm, shape, dt, offset=off)
        off += sz
    assert off <= base + 2 * NIT * 1024, (off, base)
    if l == 1:
        for nm, shape, dt, sz in DQ_BUF_SPECS:
            cx.sb[nm] = cx.nc.alloc_sbuf_tensor_at(nm, shape, dt, offset=off)
            off += sz
        assert off <= base + 2 * NIT * 1024, (off, base)


def emit_ones_f32(cx):
    pg = cx.pg
    t = cx.sb["onescf"]
    pg.op("pool", (lambda e: e.memset(t[:], 1.0)), reads=[], writes=["onescf"],
          tag="onescf")
    t2 = cx.sb["onesrf"]
    pg.op("pool", (lambda e: e.memset(t2[:], 1.0)), reads=[], writes=["onesrf"],
          tag="onesrf")
    t3 = cx.sb["onessq"]
    pg.op("pool", (lambda e: e.memset(t3[:], 1.0)), reads=[], writes=["onessq"],
          tag="onessq")
    t4 = cx.sb["lsel"]
    pg.op("pool", (lambda e: e.memset(t4[:], 0.0)), reads=[], writes=["lsel"],
          tag="lsel0")
    pg.op("pool", (lambda e: e.memset(t4[:, 0:1], 1.0)), reads=["lsel"],
          writes=["lsel"], tag="lsel1")
    pg.op("pool", (lambda e: e.memset(t4[:, 3:4], 1.0)), reads=["lsel"],
          writes=["lsel"], tag="lsel2")
    t5 = cx.sb["onesbf"]
    pg.op("pool", (lambda e: e.memset(t5[:], 1.0)), reads=[], writes=["onesbf"],
          tag="onesbf")


def build_segment_B(l):
    cx = new_ctx()
    nc = cx.nc
    need = {"c_cosd", "c_sinds", "c_masks", "c_identf", "c_onesb", "c_onesc", "c_cvec"}
    if l == 1:
        need.add("c_sel32h")
    decl_common(cx, const_names=need)
    dram(cx, "xn_all", [4 * D, TPC], BF16, kind="ExternalInput")
    dram(cx, "attn_my", [S, TPC], BF16, kind="ExternalOutput")
    dram(cx, "q_scr", [HPC * 128, S], BF16)
    dram(cx, "k_scr", [KVPC * 128, S], BF16)
    dram(cx, "v_scr", [S, KVPC * 128], BF16)
    if l == 0:
        dram(cx, "l0_wq", [D, 1024], F32, kind="ExternalInput")
        dram(cx, "l0_wk", [D, 256], F32, kind="ExternalInput")
        dram(cx, "l0_wv", [D, 256], F32, kind="ExternalInput")
    else:
        dram(cx, "l1_q_cd", [D, 1024], I32, kind="ExternalInput")
        dram(cx, "l1_k_cd", [D, 256], I32, kind="ExternalInput")
        dram(cx, "l1_v_cd", [D, 256], I32, kind="ExternalInput")
        dram(cx, "l1_q_sc", [32, 1024], F32, kind="ExternalInput")
        dram(cx, "l1_k_sc", [32, 256], F32, kind="ExternalInput")
        dram(cx, "l1_v_sc", [32, 256], F32, kind="ExternalInput")
        dram(cx, "wq_deq", [D, 1024], BF16)
        dram(cx, "wk_deq", [D, 256], BF16)
        dram(cx, "wv_deq", [D, 256], BF16)
    decl_segB(cx, l)
    emit_ones_f32(cx)
    wg_keys = [("qk", "wg", 0), ("qk", "wg", 1)]
    if l == 1:
        emit_dequant_qkv_to_dram(cx, "dq")
        cx.pg.fence([("dq", "sc"), ("dq", "scf"), ("dq", "cd", 0), ("dq", "cd", 1)]
                    + [("dq", "wdq", j) for j in range(3)] + wg_keys,
                    tag="dqfence")
    emit_qkv(cx, l, "qk")
    at_keys = (wg_keys + [("at", "q", 0), ("at", "q", 1), ("at", "kbuf", 0),
                          ("at", "kbuf", 1), ("at", "vbuf"), ("at", "lnl"),
                          ("at", "rinv"), ("at", "ast", 0), ("at", "ast", 1)]
               + [("at", "p", j) for j in range(4)])
    cx.pg.fence(at_keys, tag="atfence")
    emit_attn(cx, "at")
    with nc.Block() as block:
        cx.pg.finalize(nc, block)
    nc.finalize()
    return cx





def emit_castw_to_dram(cx, tag, src_name, dst_name):
    pg = cx.pg
    st = pg.stream("wcst", "pool", depth=4)
    s = cx.dr[src_name]
    d = cx.dr[dst_name]
    for it in range(NIT):
        pg.dma(st, (lambda e, i=it: e.dma_start(
            out=d.ap()[i * 128:(i + 1) * 128, :],
            in_=s.ap()[i * 128:(i + 1) * 128, :])),
            reads=[src_name], writes=[dst_name], tag=f"{tag}c{it}")


def emit_scale_hilo(cx, tag, src_name, rows, cols, load_selblk=False):
    pg = cx.pg
    hi = cx.sb["p_schi"]
    ld = pg.stream("scl", "pool", depth=2)
    if load_selblk:
        sb_d = cx.dr["c_selblk"]
        pg.dma(ld, (lambda e: e.dma_start(out=cx.sb["selblk"][:], in_=sb_d.ap())),
               reads=["c_selblk"], writes=["selblk"], tag=f"{tag}sld")
    d = cx.dr[src_name]
    pg.dma(ld, (lambda e: e.dma_start(out=hi[0:rows, 0:cols], in_=d.ap())),
           reads=[src_name], writes=[(tag, "schi")], tag=f"{tag}lh")


def make_dequant_chunks(cx, tag, codes_name, dst_name, kind):
    pg = cx.pg
    selname = "selblk" if kind == "i8" else "sel32h"
    selrows = 128 if kind == "i8" else 32
    bias = -128.0 if kind == "i8" else -8.0
    hi = cx.sb["p_schi"]
    cdb = cx.sb["p_codes"]
    wdq = [cx.sb[f"p_wdq{j}"] for j in range(3)]
    ldcd = pg.stream("pcd", "sp", depth=4)
    stw = pg.stream("pwd", "act", depth=6)
    s = cx.dr[codes_name]
    d = cx.dr[dst_name]
    state = {"nb": 0}

    def emit_cd(grp):
        it, hh = grp // 2, grp % 2
        half = cdb[:, hh * 2048:(hh + 1) * 2048]
        pg.dma(ldcd, (lambda e, a=half, i=it, h_=hh: e.dma_start(
            out=a, in_=s.ap()[i * 128:(i + 1) * 128,
                              h_ * 2048:(h_ + 1) * 2048])),
            reads=[codes_name], writes=[(tag, "cd", hh)],
            tag=f"{tag}cd{it}_{hh}")

    def step(idx):
        grp, ob = idx // 4, idx % 4
        it, hh = grp // 2, grp % 2
        if ob == 0:
            if grp == 0:
                emit_cd(0)
            if grp + 1 < 2 * NIT:
                emit_cd(grp + 1)
        sel = cx.sb[selname]
        half = cdb[:, hh * 2048:(hh + 1) * 2048]
        rch = (tag, "cd", hh)
        oc = hh * 2048 + ob * 512
        nb = state["nb"]
        wb = wdq[nb % 3]
        rwb = (tag, "wdq", nb % 3)
        ps_sc = cx.ps[7]
        pbk = ("PB", 7)
        state["nb"] += 1
        lhs_sl = sel[:, it * 128:(it + 1) * 128]
        pg.op("pe", (lambda e, ls=lhs_sl, o=oc, p=ps_sc:
                     e.matmul(p[:, 0:512], ls[0:selrows, :],
                              hi[0:selrows, o:o + 512],
                              start=True, stop=True)),
              reads=[(tag, "schi"), selname], writes=[pbk],
              tag=f"{tag}mh{it}_{oc}")
        pg.op("dve", (lambda e, a=wb, c_=half, o=ob * 512, p=ps_sc:
                      e.scalar_tensor_tensor(a[:], c_[:, o:o + 512], bias,
                                             p[:, 0:512],
                                             ALU.add, ALU.mult)),
              reads=[rch, pbk], writes=[rwb], tag=f"{tag}dq{it}_{oc}")
        pg.dma(stw, (lambda e, a=wb, i=it, o=oc: e.dma_start(
            out=d.ap()[i * 128:(i + 1) * 128, o:o + 512], in_=a[:])),
            reads=[rwb], writes=[dst_name], tag=f"{tag}st{it}_{oc}")
    return [lambda i=i: step(i) for i in range(NIT * 8)]


def mk_wcb(steps):
    q = list(steps)

    def cb(_=None):
        if q:
            q.pop(0)()
    cb.flush = lambda: [q.pop(0)() for _ in range(len(q))]
    return cb


def emit_oproj(cx, tag, x_in_name, x_out_name, rhs_mode="direct",
               wname="wo_bf", wcast=False):
    pg = cx.pg
    wo = cx.dr[wname]
    xi = cx.dr[x_in_name]
    xo = cx.dr[x_out_name]
    slabs = [cx.sb["wslab0"], cx.sb["wslab1"]]
    x0b = [cx.sb[f"x0b{j}"] for j in range(8)]
    xst = [cx.sb[f"xst{j}"] for j in range(2)]
    ldw = pg.stream("wslc" if wcast else "wsl", "pool" if wcast else "sp",
                    depth=2)
    ldx = pg.stream("xin", "pool", depth=4)
    stx = pg.stream("xout", "act", depth=4)
    banks = [cx.ps[j] for j in range(8)]
    if rhs_mode == "ag":
        at = cx.dr["attn_all"]
        xb = cx.sb["xb_slab"]
        ldr = pg.stream("rhl", "sp", depth=2)

        def _ldall(e, hl):
            if getattr(cx, "rank_sv", None) is None:
                cx.rank_sv = e.value_load(cx.sb["rank_sb"][0:1, 0:1])
                cx.off_cache = {}
            off = cx.off_cache.get(hl)
            if off is None:
                off = e.compute_val(cx.rank_sv + hl * 128 * TPC)
                cx.off_cache[hl] = off
            dst = xb[:].rearrange("p (a h c) -> p a h c", h=HPC, c=TPC)[:, :, hl, :]
            return e.dma_start(
                out=dst,
                in_=bass.AP(at, off, [[TPC, 128], [TPC * TPC, 4], [1, TPC]]))
        for hl in range(HPC):
            pg.dma(ldr, (lambda e, h_=hl: _ldall(e, h_)),
                   reads=["attn_all", ("rank_sb",)],
                   writes=[(tag, "rhsall")], tag=f"{tag}rall{hl}")

        def rhs_ap(hd, tb):
            return cx.sb["xb_slab"][:, hd * TPC + tb * 512:hd * TPC + (tb + 1) * 512]
        rhs_reads = lambda hd: [(tag, "rhsall")]
    else:
        at = cx.dr["attn_tok"]
        rbufs = [cx.sb["rhs0"], cx.sb["rhs1"]]
        ldr = pg.stream("rhl", "sp", depth=3)
    for os_ in range(8):
        wsl = slabs[os_ % 2]
        rw = (tag, "wsl", os_ % 2)
        pg.dma(ldw, (lambda e, w_=wsl, o=os_: e.dma_start(
            out=w_[:].rearrange("p (a c) -> p a c", c=512),
            in_=wo.ap()[:, o * 512:(o + 1) * 512].rearrange(
                "(a p) c -> p a c", p=128))),
            reads=[wname], writes=[rw], tag=f"{tag}w{os_}")
        for j in range(8):
            ot4, tb = j // 2, j % 2
            ot = os_ * 4 + ot4
            xb_ = x0b[j]
            pg.dma(ldx, (lambda e, a=xb_, r0=ot * 128, c0=tb * 512: e.dma_start(
                out=a[:], in_=xi.ap()[r0:r0 + 128, c0:c0 + 512])),
                reads=[("XD", x_in_name, ot, tb)], writes=[(tag, "x0", j)],
                tag=f"{tag}x{os_}_{j}")
        for hd in range(NIT):
            if rhs_mode == "direct":
                rb_ = rbufs[hd % 2]
                rr = (tag, "rhs", hd % 2)
                pg.dma(ldr, (lambda e, a=rb_, h_=hd: e.dma_start(
                    out=a[:], in_=at.ap()[h_ * 128:(h_ + 1) * 128, :])),
                    reads=["attn_tok"], writes=[rr], tag=f"{tag}r{os_}_{hd}")
            for ot4 in range(4):
                for tb in range(2):
                    if rhs_mode == "direct":
                        rh_ap = rb_[:, tb * 512:(tb + 1) * 512]
                        rds = [rw, rr]
                    else:
                        rh_ap = cx.sb["xb_slab"][:, hd * TPC + tb * 512:
                                                 hd * TPC + (tb + 1) * 512]
                        rds = [rw, (tag, "rhsall")]
                    pg.op("pe", (lambda e, p=cx.pp[ot4], w_=wsl, h_=hd, o4=ot4,
                                 t_=tb, s=(hd == 0), z=(hd == NIT - 1), r_=rh_ap:
                                 e.matmul(p[:, t_ * 512:(t_ + 1) * 512],
                                          w_[:, h_ * 512 + o4 * 128:
                                             h_ * 512 + o4 * 128 + 128],
                                          r_, start=s, stop=z)),
                          reads=rds, writes=[("PB", ot4 * 2 + tb)],
                          tag=f"{tag}mm{os_}_{hd}_{ot4}{tb}")
        for j in range(8):
            ot4, tb = j // 2, j % 2
            ot = os_ * 4 + ot4
            pb = banks[j]
            xb_ = x0b[j]
            xs = xst[j % 2]
            rxs = (tag, "xst", j % 2)
            pg.op("dve", (lambda e, p=pb, a=xs, b=xb_:
                          e.tensor_add(a[:], p[:], b[:])),
                  reads=[("PB", j), (tag, "x0", j)], writes=[rxs],
                  tag=f"{tag}ev{os_}_{j}")
            pg.dma(stx, (lambda e, a=xs, r0=ot * 128, c0=tb * 512: e.dma_start(
                out=xo.ap()[r0:r0 + 128, c0:c0 + 512], in_=a[:])),
                reads=[rxs], writes=[("XD", x_out_name, ot, tb)],
                tag=f"{tag}sx{os_}_{j}")


def emit_gates(cx, tag):
    pg = cx.pg
    gsl = cx.sb["gates_slab"]
    xb = cx.sb["xb_slab"]
    qk = cx.sb["qk_sb"]
    hsb = cx.sb["h_sb"]
    prod = cx.sb["prod_sb"]
    sc8 = cx.sb["sc8"]
    g = cx.sb["g_sb"]
    gslab = cx.sb["g_slab"]
    identf = cx.sb["identf"]
    ld = pg.stream("gld", "pool", depth=2)
    gates = cx.dr["l1f_gates"]
    pg.dma(ld, (lambda e: e.dma_start(
        out=gsl[:].rearrange("p (a c) -> p a c", c=640),
        in_=gates.ap().rearrange("(a p) c -> p a c", p=128))),
        reads=["l1f_gates"], writes=[(tag, "gsl")], tag=f"{tag}ld")
    ps1, ps2 = cx.ps[0], cx.ps[1]
    ps_tr = cx.ps[2]
    for tt in range(8):
        lhs_of = tt * 128
        for it in range(NIT):
            lhs = xb[:, it * 1024 + lhs_of:it * 1024 + lhs_of + 128]
            pg.op("pe", (lambda e, a=lhs, i=it, s=(it == 0), z=(it == NIT - 1):
                         e.matmul(ps1[:], a, gsl[:, i * 640:i * 640 + 512],
                                  start=s, stop=z)),
                  reads=[("xb",), (tag, "gsl")], writes=[("PB", 0)],
                  tag=f"{tag}m1{tt}_{it}")
            pg.op("pe", (lambda e, a=lhs, i=it, s=(it == 0), z=(it == NIT - 1):
                         e.matmul(ps2[:, 0:128], a,
                                  gsl[:, i * 640 + 512:i * 640 + 640],
                                  start=s, stop=z)),
                  reads=[("xb",), (tag, "gsl")], writes=[("PB", 1)],
                  tag=f"{tag}m2{tt}_{it}")
        pg.op("dve", (lambda e: e.tensor_copy(qk[:], ps1[:])),
              reads=[("PB", 0)], writes=[(tag, "qk")], tag=f"{tag}eq{tt}")
        pg.op("dve", (lambda e: e.tensor_copy(hsb[:], ps2[:, 0:128])),
              reads=[("PB", 1)], writes=[(tag, "h")], tag=f"{tag}eh{tt}")
        pg.op("dve", (lambda e: e.tensor_mul(prod[:], qk[:, 0:256], qk[:, 256:512])),
              reads=[(tag, "qk")], writes=[(tag, "prod")], tag=f"{tag}pr{tt}")
        pg.op("dve", (lambda e: e.reduce_sum(
            sc8[:, 0:8].rearrange("p (e o) -> p e o", o=1),
            prod[:].rearrange("p (e d) -> p e d", d=32), axis=AX.X)),
              reads=[(tag, "prod")], writes=[(tag, "sc8")], tag=f"{tag}rs{tt}")
        pg.op("dve", (lambda e: e.reduce_max(sc8[:, 8:9], sc8[:, 0:8], axis=AX.X)),
              reads=[(tag, "sc8")], writes=[(tag, "sc8")], tag=f"{tag}mx{tt}")
        pg.op("dve", (lambda e: e.tensor_scalar(sc8[:, 9:10], sc8[:, 8:9],
                                                -1.0 / math.sqrt(32.0), None,
                                                ALU.mult)),
              reads=[(tag, "sc8")], writes=[(tag, "sc8")], tag=f"{tag}nb{tt}")
        pg.op("act", (lambda e: e.activation(sc8[:, 16:24], sc8[:, 0:8], AF.Exp,
                                             bias=sc8[:, 9:10], scale=cv(cx, 4))),
              reads=[(tag, "sc8"), "cvec"], writes=[(tag, "sc8")], tag=f"{tag}ex{tt}")
        pg.op("dve", (lambda e: e.reduce_sum(sc8[:, 10:11], sc8[:, 16:24], axis=AX.X)),
              reads=[(tag, "sc8")], writes=[(tag, "sc8")], tag=f"{tag}sm{tt}")
        pg.op("dve", (lambda e: e.reciprocal(sc8[:, 11:12], sc8[:, 10:11])),
              reads=[(tag, "sc8")], writes=[(tag, "sc8")], tag=f"{tag}rc{tt}")
        pg.op("dve", (lambda e: e.tensor_scalar(sc8[:, 16:24], sc8[:, 16:24],
                                                sc8[:, 11:12], None, ALU.mult)),
              reads=[(tag, "sc8")], writes=[(tag, "sc8")], tag=f"{tag}w8{tt}")
        def gmul(e):
            h3 = hsb[:].rearrange("p (e r) -> p e r", r=16)
            w3 = sc8[:, 16:24].rearrange("p (e o) -> p e o", o=1)
            g3 = g[:].rearrange("p (e r) -> p e r", r=16)
            h3b, w3b = bass.broadcast_tensor_aps(h3, w3)
            return e.tensor_tensor(g3, h3b, w3b, ALU.mult)
        pg.op("dve", gmul, reads=[(tag, "h"), (tag, "sc8")], writes=[(tag, "g")],
              tag=f"{tag}g{tt}")
        pg.op("pe", (lambda e: e.transpose(ps_tr[:, 0:128], g[:], identf[:])),
              reads=[(tag, "g"), "identf"], writes=[("PB", 2)], tag=f"{tag}tr{tt}")
        pg.op("dve", (lambda e, o=tt * 128: e.tensor_copy(gslab[:, o:o + 128],
                                                          ps_tr[:, 0:128])),
              reads=[("PB", 2)], writes=[(tag, "gslab")], tag=f"{tag}gc{tt}")


def emit_ffn(cx, l, tag, x_in_name, x_out_name, wname="w1_bf", wcast=False):
    pg = cx.pg
    w1 = cx.dr[wname]
    xi = cx.dr[x_in_name]
    xo = cx.dr[x_out_name]
    xb = cx.sb["xb_slab"]
    slabs = [cx.sb["wslab0"], cx.sb["wslab1"]]
    x0b = [cx.sb[f"x0b{j}"] for j in range(8)]
    xst = [cx.sb[f"xst{j}"] for j in range(2)]
    ldw = pg.stream("wslc" if wcast else "wsl", "pool" if wcast else "sp",
                    depth=2)
    ldx = pg.stream("xin", "pool", depth=4)
    stx = pg.stream("xout", "act", depth=4)
    banks = [cx.ps[j] for j in range(8)]
    if l == 1:
        bsb = cx.sb["b_sb"]
        bmt = cx.sb["bmt_sb"]
        gslab = cx.sb["g_slab"]
    for os_ in range(8):
        wsl = slabs[os_ % 2]
        rw = (tag, "wsl", os_ % 2)
        pg.dma(ldw, (lambda e, w_=wsl, o=os_: e.dma_start(
            out=w_[:].rearrange("p (a c) -> p a c", c=512),
            in_=w1.ap()[:, o * 512:(o + 1) * 512].rearrange(
                "(a p) c -> p a c", p=128))),
            reads=[wname], writes=[rw], tag=f"{tag}w{os_}")
        for j in range(8):
            ot4, tb = j // 2, j % 2
            ot = os_ * 4 + ot4
            xb_ = x0b[j]
            pg.dma(ldx, (lambda e, a=xb_, r0=ot * 128, c0=tb * 512: e.dma_start(
                out=a[:], in_=xi.ap()[r0:r0 + 128, c0:c0 + 512])),
                reads=[("XD", x_in_name, ot, tb)], writes=[(tag, "x0", j)],
                tag=f"{tag}x{os_}_{j}")
        for it in range(NIT):
            for ot4 in range(4):
                for tb in range(2):
                    last = (it == NIT - 1) and l == 0
                    pg.op("pe", (lambda e, p=cx.pp[ot4], w_=wsl, i=it, o4=ot4,
                                 t_=tb, s=(it == 0), z=last:
                                 e.matmul(p[:, t_ * 512:(t_ + 1) * 512],
                                          w_[:, i * 512 + o4 * 128:
                                             i * 512 + o4 * 128 + 128],
                                          xb[:, i * 1024 + t_ * 512:
                                             i * 1024 + (t_ + 1) * 512],
                                          start=s, stop=z)),
                          reads=[rw, ("xb",)], writes=[("PB", ot4 * 2 + tb)],
                          tag=f"{tag}mm{os_}_{it}_{ot4}{tb}")
        if l == 1:
            for ot4 in range(4):
                for tb in range(2):
                    pb = banks[ot4 * 2 + tb]
                    ot = os_ * 4 + ot4
                    pg.op("pe", (lambda e, p=pb, o=ot * 128, t_=tb:
                                 e.matmul(p[:], bmt[:, o:o + 128],
                                          gslab[:, t_ * 512:(t_ + 1) * 512],
                                          start=False, stop=True)),
                          reads=[("bmt",), ("gt", "gslab")],
                          writes=[("PB", ot4 * 2 + tb)],
                          tag=f"{tag}bm{os_}_{ot4}{tb}")
        for j in range(8):
            ot4, tb = j // 2, j % 2
            ot = os_ * 4 + ot4
            pb = banks[j]
            xb_ = x0b[j]
            xs = xst[j % 2]
            rxs = (tag, "xst", j % 2)
            if l == 1:
                pg.op("dve", (lambda e, p=pb, a=xs, b=xb_, o=ot:
                              e.scalar_tensor_tensor(a[:], p[:], bsb[:, o:o + 1],
                                                     b[:], ALU.add, ALU.add)),
                      reads=[("PB", j), (tag, "x0", j), ("bsb",)],
                      writes=[rxs], tag=f"{tag}ev{os_}_{j}")
            else:
                pg.op("dve", (lambda e, p=pb, a=xs, b=xb_:
                              e.tensor_add(a[:], p[:], b[:])),
                      reads=[("PB", j), (tag, "x0", j)], writes=[rxs],
                      tag=f"{tag}ev{os_}_{j}")
            pg.dma(stx, (lambda e, a=xs, r0=ot * 128, c0=tb * 512: e.dma_start(
                out=xo.ap()[r0:r0 + 128, c0:c0 + 512], in_=a[:])),
                reads=[rxs], writes=[("XD", x_out_name, ot, tb)],
                tag=f"{tag}sx{os_}_{j}")


PREP_SPECS = [
    ("p_codes", [128, 4096], I32, 16384),
    ("p_scf", [128, 4096], F32, 16384),
    ("p_schi", [128, 4096], BF16, 8192),
    ("p_sclo", [128, 4096], BF16, 8192),
    ("p_wdq0", [128, 512], BF16, 1024), ("p_wdq1", [128, 512], BF16, 1024),
    ("p_wdq2", [128, 512], BF16, 1024),
]
GATE_SPECS = [
    ("gates_slab", [128, 32 * 640], BF16, 40960),
    ("qk_sb", [128, 512], F32, 2048),
    ("h_sb", [128, 128], F32, 512),
    ("prod_sb", [128, 256], F32, 1024),
    ("g_sb", [128, 128], F32, 512),
]
PREP_KEYS = [("pw", "cd", 0), ("pw", "cd", 1), ("pw", "scf"), ("pw", "schi"),
             ("pw", "sclo")] + [("pw", "wdq", j) for j in range(3)]
GATE_KEYS = [("gt", "gsl"), ("gt", "qk"), ("gt", "h"), ("gt", "prod"),
             ("gt", "sc8"), ("gt", "g")]


def decl_segC(cx, l):
    if l == 0:
        sb_alloc(cx, "selblk", [128, 4096], BF16)
    sb_alloc(cx, "xb_slab", [128, NIT * 1024], BF16)
    sb_alloc(cx, "wslab0", [128, NIT * 512], BF16)
    sb_alloc(cx, "wslab1", [128, NIT * 512], BF16)
    sb_alloc(cx, "rhs0", [128, 1024], BF16)
    sb_alloc(cx, "rhs1", [128, 1024], BF16)
    for j in range(8):
        sb_alloc(cx, f"x0b{j}", [128, 512], F32)
    for j in range(2):
        sb_alloc(cx, f"xst{j}", [128, 512], F32)
    sb_alloc(cx, "xtile0", [128, 1024], F32)
    sb_alloc(cx, "xtile1", [128, 1024], F32)
    sb_alloc(cx, "sq0", [128, 1024], BF16)
    sb_alloc(cx, "sq1", [128, 1024], BF16)
    sb_alloc(cx, "stats", [1, 3072], F32)
    sb_alloc(cx, "srow", [1, 2048], BF16)
    sb_alloc(cx, "bct", [128, 2048], BF16)
    if l == 1:
        sb_alloc(cx, "b_sb", [128, 32], F32)
        sb_alloc(cx, "bmt_sb", [128, 4096], BF16)
        sb_alloc(cx, "g_slab", [128, 1024], BF16)
        sb_alloc(cx, "sc8", [128, 32], F32)
    base = cx.nc.lookup_mloc(cx.sb["wslab0"]).addr
    off = base
    for nm, shape, dt, sz in PREP_SPECS:
        cx.sb[nm] = cx.nc.alloc_sbuf_tensor_at(nm, shape, dt, offset=off)
        off += sz
    assert off <= base + 2 * NIT * 1024, (off, base)
    if l == 1:
        off = base
        for nm, shape, dt, sz in GATE_SPECS:
            if nm in cx.sb and nm in ("g_slab",):
                continue
            cx.sb[nm] = cx.nc.alloc_sbuf_tensor_at(nm, shape, dt, offset=off)
            off += sz
        assert off <= base + 2 * NIT * 1024, (off, base)


WSLAB_KEYS = [("op", "wsl", 0), ("op", "wsl", 1), ("ff", "wsl", 0),
              ("ff", "wsl", 1)]


def emit_segC_body(cx, l):
    pg = cx.pg
    if l == 0:
        emit_castw_to_dram(cx, "cwo", "l0_wo", "wo_bf")
        emit_scale_hilo(cx, "pw", "l0f_sc", 128, 4096)
        emit_dequant_big(cx, "pw", "l0f_cd", "w1_bf", "i8")
    else:
        emit_scale_hilo(cx, "pw", "l1_o_sc", 32, 4096)
        emit_dequant_big(cx, "pw", "l1_o_cd", "wo_bf", "f4")
        emit_castw_to_dram(cx, "cw1", "l1f_WT", "w1_bf")
    pg.fence(PREP_KEYS + WSLAB_KEYS, tag="prepfence")
    if l == 1:
        ldc = pg.stream("cld2", "pool", depth=2)
        b_d = cx.dr["l1f_b"]
        bm_d = cx.dr["l1f_BmT"]
        pg.dma(ldc, (lambda e: e.dma_start(out=cx.sb["b_sb"][:], in_=b_d.ap())),
               reads=["l1f_b"], writes=[("bsb",)], tag="ldb")
        pg.dma(ldc, (lambda e: e.dma_start(out=cx.sb["bmt_sb"][:], in_=bm_d.ap())),
               reads=["l1f_BmT"], writes=[("bmt",)], tag="ldbmt")
    emit_oproj(cx, "op", "x_in", "x_mid")
    emit_ln_standalone2(cx, "x_mid", None, "ln1")
    if l == 1:
        pg.fence(WSLAB_KEYS + GATE_KEYS, tag="gatefence")
        emit_gates(cx, "gt")
        pg.fence(WSLAB_KEYS + GATE_KEYS, tag="gatefence2")
    emit_ffn(cx, l, "ff", "x_mid", "x_out")
    if l == 0:
        emit_ln_standalone2(cx, "x_out", "xn_out", "ln2")


def emit_ln_standalone2(cx, x_src_name, xn_dst_name, tag):
    pg = cx.pg
    x_src = cx.dr[x_src_name]
    xn_dst = cx.dr[xn_dst_name] if xn_dst_name else None
    ld = pg.stream("xload", "sp", depth=6)
    st_out = pg.stream("agout", "act", depth=6)
    xb = cx.sb["xb_slab"]
    xts = [cx.sb["xtile0"], cx.sb["xtile1"]]
    sqs = [cx.sb["sq0"], cx.sb["sq1"]]
    for it in range(NIT):
        xt = xts[it % 2]
        sq_t = sqs[it % 2]
        rx = ("LN", "xt", it % 2)
        rsq = ("LN", "sq", it % 2)
        pg.dma(ld, (lambda e, a=xt, b=x_src, i=it:
                    e.dma_start(out=a[:], in_=b.ap()[i * 128:(i + 1) * 128, :])),
               reads=[("XD", x_src_name, it, 0), ("XD", x_src_name, it, 1)],
               writes=[rx], tag=f"{tag}ldx{it}")
        xbv = xb[:, it * 1024:(it + 1) * 1024]
        pg.op("dve", (lambda e, a=xbv, b=xt: e.tensor_copy(a, b[:])),
              reads=[rx], writes=[("xb",)], tag=f"{tag}cast{it}")
        pg.op("act", (lambda e, a=sq_t, b=xt: e.activation(a[:], b[:], AF.Square)),
              reads=[rx], writes=[rsq], tag=f"{tag}sq{it}")
        emit_ln_tile_stats(cx, tag, it, ("xb",), xbv, rsq, sq_t[:],
                           first=(it == 0), last=(it == NIT - 1))
    rbct = emit_ln_finalize(cx, tag)
    for it in range(NIT):
        xbv = xb[:, it * 1024:(it + 1) * 1024]
        emit_ln_apply(cx, tag, it, xbv, ("xb",), rbct, xn_dst_name, xn_dst, st_out)


def build_segment_C(l):
    cx = new_ctx()
    nc = cx.nc
    need = {"c_identf", "c_onesb", "c_onesc", "c_cvec"}
    need.add("c_selblk" if l == 0 else "c_sel32h")
    decl_common(cx, const_names=need)
    dram(cx, "attn_tok", [S, TPC], BF16, kind="ExternalInput")
    dram(cx, "x_in", [D, TPC], F32, kind="ExternalInput")
    dram(cx, "x_mid", [D, TPC], F32)
    dram(cx, "x_out", [D, TPC], F32, kind="ExternalOutput")
    dram(cx, "wo_bf", [D, D], BF16)
    dram(cx, "w1_bf", [D, D], BF16)
    if l == 0:
        dram(cx, "xn_out", [D, TPC], BF16, kind="ExternalOutput")
        dram(cx, "l0_wo", [D, D], F32, kind="ExternalInput")
        dram(cx, "l0f_cd", [D, D], I32, kind="ExternalInput")
        dram(cx, "l0f_sc", [128, D], F32, kind="ExternalInput")
    else:
        dram(cx, "l1_o_cd", [D, D], I32, kind="ExternalInput")
        dram(cx, "l1_o_sc", [32, D], F32, kind="ExternalInput")
        dram(cx, "l1f_WT", [D, D], F32, kind="ExternalInput")
        dram(cx, "l1f_b", [128, 32], F32, kind="ExternalInput")
        dram(cx, "l1f_gates", [D, 640], F32, kind="ExternalInput")
        dram(cx, "l1f_BmT", [128, D], F32, kind="ExternalInput")
    decl_segC(cx, l)
    emit_segC_body(cx, l)
    with nc.Block() as block:
        cx.pg.finalize(nc, block)
    nc.finalize()
    return cx







P1K = ([("dq", "sc"), ("dq", "scf"), ("dq", "cd", 0), ("dq", "cd", 1)]
       + [("dq", "wdq", j) for j in range(3)])
P2K = ([("at", "q", 0), ("at", "q", 1), ("at", "kbuf", 0), ("at", "kbuf", 1),
        ("at", "vbuf"),
        ("at", "lnl"), ("at", "rinv"), ("at", "ast", 0), ("at", "ast", 1),
        "masks"]
       + [("at", "p", j) for j in range(4)])
P3K = PREP_KEYS + ["selblk"]
P4K = [("LN", "xt", 0), ("LN", "xt", 1), ("LN", "sq", 0), ("LN", "sq", 1),
       ("LN", "stats"), ("LN", "srow"), ("LN", "bct")]
P5K = GATE_KEYS
WGK = [("qk", "wu", 0), ("qk", "wu", 1), ("qk", "wu", 2),
       ("op", "wsl", 0), ("op", "wsl", 1),
       ("ff", "wsl", 0), ("ff", "wsl", 1)]
FENCE_ALL = (P1K + P2K + P3K + P4K + P5K + WGK
             + [("qk", "xn"), ("xb",), ("op", "rhsall")])

RG = [[0, 1, 2, 3], [4, 5, 6, 7]]

XB_PREP_SPECS = PREP_SPECS + [("selblk", [128, 4096], BF16, 8192)]

POOL_SETS = {
    "P6": [("wu0", [128, 4096], BF16, 8192), ("wu1", [128, 4096], BF16, 8192),
           ("wu2", [128, 4096], BF16, 8192)],
    "P1": DQ_BUF_SPECS,
    "P2": ATTN_BUF_SPECS + [("masks", [128, 4096], BF16, 8192)],
    "P4": [("xtile0", [128, 1024], F32, 4096), ("xtile1", [128, 1024], F32, 4096),
           ("sq0", [128, 1024], BF16, 2048), ("sq1", [128, 1024], BF16, 2048),
           ("stats", [1, 3072], F32, 12288), ("srow", [1, 2048], BF16, 4096),
           ("bct", [128, 2048], BF16, 4096)],
    "P5": GATE_SPECS,
}


def decl_fused(cx):
    nc = cx.nc
    decl_common(cx, const_names={"c_cosd", "c_sinds", "c_masks", "c_selblk",
                                 "c_sel32h", "c_identf", "c_onesb", "c_onesc",
                                 "c_cvec"})
    dram(cx, "x_fm", [D, TPC], F32, kind="ExternalInput")
    dram(cx, "rank_base", [1, 1], I32, kind="ExternalInput")
    dram(cx, "l0_wq", [D, 1024], F32, kind="ExternalInput")
    dram(cx, "l0_wk", [D, 256], F32, kind="ExternalInput")
    dram(cx, "l0_wv", [D, 256], F32, kind="ExternalInput")
    dram(cx, "l0_wo", [D, D], F32, kind="ExternalInput")
    dram(cx, "l0f_cd", [D, D], I32, kind="ExternalInput")
    dram(cx, "l0f_sc", [128, D], F32, kind="ExternalInput")
    dram(cx, "l1_q_cd", [D, 1024], I32, kind="ExternalInput")
    dram(cx, "l1_k_cd", [D, 256], I32, kind="ExternalInput")
    dram(cx, "l1_v_cd", [D, 256], I32, kind="ExternalInput")
    dram(cx, "l1_q_sc", [32, 1024], F32, kind="ExternalInput")
    dram(cx, "l1_k_sc", [32, 256], F32, kind="ExternalInput")
    dram(cx, "l1_v_sc", [32, 256], F32, kind="ExternalInput")
    dram(cx, "l1_o_cd", [D, D], I32, kind="ExternalInput")
    dram(cx, "l1_o_sc", [32, D], F32, kind="ExternalInput")
    dram(cx, "l1f_WT", [D, D], F32, kind="ExternalInput")
    dram(cx, "l1f_b", [128, 32], F32, kind="ExternalInput")
    dram(cx, "l1f_gates", [D, 640], F32, kind="ExternalInput")
    dram(cx, "l1f_BmT", [128, D], F32, kind="ExternalInput")
    dram(cx, "out_fm", [D, TPC], F32, kind="ExternalOutput")
    dram(cx, "ag_in", [D, TPC], BF16)
    dram(cx, "xn_all", [4 * D, TPC], BF16)
    dram(cx, "a2a_in", [S, TPC], BF16)
    dram(cx, "attn_all", [4 * S, TPC], BF16)
    dram(cx, "q_scr", [HPC * 128, S], BF16)
    dram(cx, "k_scr", [KVPC * 128, S], BF16)
    dram(cx, "v_scr", [S, KVPC * 128], BF16)
    dram(cx, "wq_deq", [D, 1024], BF16)
    dram(cx, "wk_deq", [D, 256], BF16)
    dram(cx, "wv_deq", [D, 256], BF16)
    dram(cx, "wo_bf", [D, D], BF16)
    dram(cx, "w1_bf", [D, D], BF16)
    dram(cx, "x_mid", [D, TPC], F32)
    dram(cx, "x2_scr", [D, TPC], F32)
    sb_alloc(cx, "xb_slab", [128, NIT * 1024], BF16)
    sb_alloc(cx, "wgrp0", [128, NIT * 512], BF16)
    sb_alloc(cx, "wgrp1", [128, NIT * 512], BF16)
    cx.sb["wslab0"] = cx.sb["wgrp0"]
    cx.sb["wslab1"] = cx.sb["wgrp1"]
    for si in range(2):
        sb_alloc(cx, f"qstage{si}", [128, 512], F32)
        sb_alloc(cx, f"qswap{si}", [128, 512], F32)
        sb_alloc(cx, f"qrot{si}", [128, 512], BF16)
        sb_alloc(cx, f"utmp{si}", [128, 1024], F32)
    sb_alloc(cx, "vstage", [128, 512], F32)
    sb_alloc(cx, "vbf", [128, 512], BF16)
    sb_alloc(cx, "onescf", [128, 1], F32)
    sb_alloc(cx, "onesrf", [1, 128], F32)
    sb_alloc(cx, "onessq", [128, 128], F32)
    sb_alloc(cx, "lsel", [128, 4], BF16)
    sb_alloc(cx, "onesbf", [33, 128], F32)
    for j in range(8):
        sb_alloc(cx, f"x0b{j}", [128, 512], F32)
    for j in range(2):
        sb_alloc(cx, f"xst{j}", [128, 512], F32)
    sb_alloc(cx, "b_sb", [128, 32], F32)
    sb_alloc(cx, "bmt_sb", [128, 4096], BF16)
    sb_alloc(cx, "g_slab", [128, 1024], BF16)
    sb_alloc(cx, "sc8", [128, 32], F32)
    sb_alloc(cx, "rank_sb", [1, 2], I32)
    base = nc.lookup_mloc(cx.sb["wgrp0"]).addr
    for pname, specs in POOL_SETS.items():
        off = base + (32768 if pname == "P1" else 0)
        for nm, shape, dt, sz in specs:
            cx.sb[nm] = nc.alloc_sbuf_tensor_at(nm, shape, dt, offset=off)
            off += sz
        assert off <= base + 2 * NIT * 1024, (off, base)
    xbase = nc.lookup_mloc(cx.sb["xb_slab"]).addr
    off = xbase
    for nm, shape, dt, sz in XB_PREP_SPECS:
        cx.sb[nm] = nc.alloc_sbuf_tensor_at(nm, shape, dt, offset=off)
        off += sz
    assert off <= xbase + NIT * 2048, (off, xbase)


def build_fused():
    cx = new_ctx()
    nc = cx.nc
    pg = cx.pg
    decl_fused(cx)
    emit_ones_f32(cx)
    rkld = pg.stream("rkld", "sp", depth=1)
    rb_d = cx.dr["rank_base"]
    pg.dma(rkld, (lambda e: e.dma_start(out=cx.sb["rank_sb"][0:1, 0:1],
                                        in_=rb_d.ap())),
           reads=["rank_base"], writes=[("rank_sb",)], tag="ldrank")

    def fence():
        pg.fence(FENCE_ALL, tag="F")

    def agx():
        ag_in, xn_all = cx.dr["ag_in"], cx.dr["xn_all"]
        for f in range(4):
            pg.cc((lambda e, f_=f: e.collective_compute(
                "AllGather", ALU.bypass, replica_groups=RG,
                ins=[ag_in.ap()[f_ * 1024:(f_ + 1) * 1024, :].opt()],
                outs=[xn_all.ap()[f_ * D:(f_ + 1) * D, :].opt()])),
                reads=[("XD", "ag_in", it, tb) for it in range(8 * f, 8 * f + 8)
                       for tb in range(2)],
                writes=[("xn_all", f)], tag=f"agx{f}")

    def aga(jb):
        a2a_in, attn_all = cx.dr["a2a_in"], cx.dr["attn_all"]
        pg.cc((lambda e: e.collective_compute(
            "AllGather", ALU.bypass, replica_groups=RG,
            ins=[a2a_in.ap()[jb * 1024:(jb + 1) * 1024, :].opt()],
            outs=[attn_all.ap()[jb * S:(jb + 1) * S, :].opt()])),
            reads=[("a2a", jb)], writes=["attn_all"], tag=f"aga{jb}")

    emit_ln_standalone2(cx, "x_fm", "ag_in", "lnA")
    agx()
    fence()
    emit_qkv(cx, 0, "qk")
    fence()
    emit_scale_hilo(cx, "pw", "l0f_sc", 128, 4096, load_selblk=True)
    wcb = mk_wcb(make_dequant_chunks(cx, "pw", "l0f_cd", "w1_bf", "i8"))
    emit_attn(cx, "at", dst_name="a2a_in", chunk_cc=aga, work_cb=wcb)
    wcb.flush()
    fence()
    emit_oproj(cx, "op", "x_fm", "x_mid", rhs_mode="ag",
               wname="l0_wo", wcast=True)
    fence()
    emit_ln_standalone2(cx, "x_mid", None, "ln1")
    fence()
    emit_ffn(cx, 0, "ff", "x_mid", "x2_scr")
    fence()
    emit_ln_standalone2(cx, "x2_scr", "ag_in", "ln2")
    emit_dequant_qkv_to_dram(cx, "dq")
    agx()
    fence()
    emit_qkv(cx, 1, "qk")
    fence()
    emit_scale_hilo(cx, "pw", "l1_o_sc", 32, 4096)
    wcb = mk_wcb(make_dequant_chunks(cx, "pw", "l1_o_cd", "wo_bf", "f4"))
    emit_attn(cx, "at", dst_name="a2a_in", chunk_cc=aga, work_cb=wcb)
    wcb.flush()
    fence()
    emit_oproj(cx, "op", "x2_scr", "x_mid", rhs_mode="ag")
    fence()
    emit_ln_standalone2(cx, "x_mid", None, "ln3")
    fence()
    ldc = pg.stream("cld2", "pool", depth=2)
    b_d = cx.dr["l1f_b"]
    bm_d = cx.dr["l1f_BmT"]
    pg.dma(ldc, (lambda e: e.dma_start(out=cx.sb["b_sb"][:], in_=b_d.ap())),
           reads=["l1f_b"], writes=[("bsb",)], tag="ldb")
    pg.dma(ldc, (lambda e: e.dma_start(out=cx.sb["bmt_sb"][:], in_=bm_d.ap())),
           reads=["l1f_BmT"], writes=[("bmt",)], tag="ldbmt")
    emit_gates(cx, "gt")
    fence()
    emit_ffn(cx, 1, "ff", "x_mid", "out_fm", wname="l1f_WT", wcast=True)
    with nc.Block() as block:
        pg.finalize(nc, block)
    nc.finalize()
    return cx


def kernel_fused(inputs):
    cc = np.ascontiguousarray
    x0 = inputs["L0_attn_hidden_states"]
    CON = _consts()
    cx = _build("FUSED", build_fused)
    A_ = inputs["L1_ffn_A"]
    Bm = inputs["L1_ffn_Bm"]
    gates = cc(np.concatenate([inputs["L1_ffn_Wq"].T, inputs["L1_ffn_Wk"].T,
                               np.asarray(A_).reshape(128, D).T], axis=1))
    BmT = cc(np.asarray(Bm).transpose(0, 2, 1).reshape(128, D))
    b_sb = cc(np.asarray(inputs["L1_ffn_b"]).reshape(32, 128).T)
    w1T = cc(np.asarray(inputs["L1_ffn_W"]).T)
    l0f_cd = cc(np.asarray(inputs["L0_ffn_w_q"]).T)
    l0f_sc = cc(np.asarray(inputs["L0_ffn_scales"]).T)
    Wq, Wk, Wv = inputs["L0_attn_Wq"], inputs["L0_attn_Wk"], inputs["L0_attn_Wv"]
    qc, kc, vc = (inputs["L1_attn_q_codes"], inputs["L1_attn_k_codes"],
                  inputs["L1_attn_v_codes"])
    qs, ks, vs = (inputs["L1_attn_q_scales"], inputs["L1_attn_k_scales"],
                  inputs["L1_attn_v_scales"])
    maps = []
    for c in range(NC):
        b, r = c // 4, c % 4
        m = {n: CON[n] for n in ["c_cosd", "c_sinds", "c_masks", "c_selblk",
                                 "c_sel32h", "c_identf", "c_onesb", "c_onesc",
                                 "c_cvec"]}
        m.update(
            x_fm=cc(x0[b, r * TPC:(r + 1) * TPC, :].T),
            rank_base=np.array([[r * 4 * TPC * TPC]], dtype=np.int32),
            l0_wq=cc(Wq[:, r * 1024:(r + 1) * 1024]),
            l0_wk=cc(Wk[:, r * 256:(r + 1) * 256]),
            l0_wv=cc(Wv[:, r * 256:(r + 1) * 256]),
            l0_wo=inputs["L0_attn_Wo"], l0f_cd=l0f_cd, l0f_sc=l0f_sc,
            l1_q_cd=cc(qc[:, r * 1024:(r + 1) * 1024]),
            l1_k_cd=cc(kc[:, r * 256:(r + 1) * 256]),
            l1_v_cd=cc(vc[:, r * 256:(r + 1) * 256]),
            l1_q_sc=cc(qs[:, r * 1024:(r + 1) * 1024]),
            l1_k_sc=cc(ks[:, r * 256:(r + 1) * 256]),
            l1_v_sc=cc(vs[:, r * 256:(r + 1) * 256]),
            l1_o_cd=inputs["L1_attn_o_codes"],
            l1_o_sc=inputs["L1_attn_o_scales"],
            l1f_WT=w1T, l1f_b=b_sb, l1f_gates=gates, l1f_BmT=BmT,
        )
        maps.append(m)
    res = _run(cx, maps)
    out = np.empty((2, S, D), dtype=np.float32)
    for c in range(NC):
        b, r = c // 4, c % 4
        out[b, r * TPC:(r + 1) * TPC, :] = res[c]["out_fm"].T
    return out


_CONSTS = None


def _consts():
    global _CONSTS
    if _CONSTS is None:
        _CONSTS = make_consts()
    return _CONSTS


def _const_map(names):
    c = _consts()
    return {n: c[n] for n in names}


_BUILDS = {}


def _build(name, fn, *args):
    if name not in _BUILDS:
        _BUILDS[name] = fn(*args)
    return _BUILDS[name]


def _run(cx, in_maps):
    res = run_bass_kernel_spmd(cx.nc, in_maps, core_ids=list(range(NC)))
    return res.results


def kernel(**inputs):
    return kernel_fused(inputs)


def kernel_unfused(**inputs):
    cc = np.ascontiguousarray
    x0 = inputs["L0_attn_hidden_states"]
    conA = _const_map(["c_onesb", "c_onesc", "c_cvec"])
    conB0 = _const_map(["c_cosd", "c_sinds", "c_masks", "c_identf", "c_onesb",
                        "c_onesc", "c_cvec"])
    conB1 = dict(conB0, **_const_map(["c_sel32h"]))
    conC0 = _const_map(["c_identf", "c_onesb", "c_onesc", "c_cvec", "c_selblk"])
    conC1 = _const_map(["c_identf", "c_onesb", "c_onesc", "c_cvec", "c_sel32h"])

    def core_bt(c):
        return c // 4, c % 4

    x_fm = []
    for c in range(NC):
        b, r = core_bt(c)
        x_fm.append(cc(x0[b, r * TPC:(r + 1) * TPC, :].T))

    cxA = _build("A", build_segment_A)
    resA = _run(cxA, [dict(conA, x_fm=x_fm[c]) for c in range(NC)])
    xn0 = [resA[c]["xn0"] for c in range(NC)]

    def gather_xn(xn):
        xn_all = []
        for c in range(NC):
            b, r = core_bt(c)
            xn_all.append(cc(np.concatenate([xn[4 * b + i] for i in range(4)],
                                            axis=0)))
        return xn_all

    def a2a(attn_my):
        attn_tok = []
        for c in range(NC):
            b, r = core_bt(c)
            attn_tok.append(cc(np.concatenate(
                [attn_my[4 * b + i][r * TPC:(r + 1) * TPC, :] for i in range(4)],
                axis=0)))
        return attn_tok

    Wq, Wk, Wv = inputs["L0_attn_Wq"], inputs["L0_attn_Wk"], inputs["L0_attn_Wv"]
    cxB0 = _build("B0", build_segment_B, 0)
    xn_all = gather_xn(xn0)
    maps = []
    for c in range(NC):
        b, r = core_bt(c)
        maps.append(dict(conB0, xn_all=xn_all[c],
                         l0_wq=cc(Wq[:, r * 1024:(r + 1) * 1024]),
                         l0_wk=cc(Wk[:, r * 256:(r + 1) * 256]),
                         l0_wv=cc(Wv[:, r * 256:(r + 1) * 256])))
    resB0 = _run(cxB0, maps)
    attn_tok = a2a([resB0[c]["attn_my"] for c in range(NC)])

    cxC0 = _build("C0", build_segment_C, 0)
    maps = []
    for c in range(NC):
        maps.append(dict(conC0, attn_tok=attn_tok[c], x_in=x_fm[c],
                         l0_wo=inputs["L0_attn_Wo"],
                         l0f_cd=cc(inputs["L0_ffn_w_q"].T),
                         l0f_sc=cc(inputs["L0_ffn_scales"].T)))
    resC0 = _run(cxC0, maps)
    x2 = [resC0[c]["x_out"] for c in range(NC)]
    xn2 = [resC0[c]["xn_out"] for c in range(NC)]

    cxB1 = _build("B1", build_segment_B, 1)
    xn_all = gather_xn(xn2)
    qc, kc, vc = (inputs["L1_attn_q_codes"], inputs["L1_attn_k_codes"],
                  inputs["L1_attn_v_codes"])
    qs, ks, vs = (inputs["L1_attn_q_scales"], inputs["L1_attn_k_scales"],
                  inputs["L1_attn_v_scales"])
    maps = []
    for c in range(NC):
        b, r = core_bt(c)
        maps.append(dict(conB1, xn_all=xn_all[c],
                         l1_q_cd=cc(qc[:, r * 1024:(r + 1) * 1024]),
                         l1_k_cd=cc(kc[:, r * 256:(r + 1) * 256]),
                         l1_v_cd=cc(vc[:, r * 256:(r + 1) * 256]),
                         l1_q_sc=cc(qs[:, r * 1024:(r + 1) * 1024]),
                         l1_k_sc=cc(ks[:, r * 256:(r + 1) * 256]),
                         l1_v_sc=cc(vs[:, r * 256:(r + 1) * 256])))
    resB1 = _run(cxB1, maps)
    attn_tok = a2a([resB1[c]["attn_my"] for c in range(NC)])

    cxE = _build("E", build_segment_C, 1)
    A_ = inputs["L1_ffn_A"]
    Bm = inputs["L1_ffn_Bm"]
    gates = cc(np.concatenate([inputs["L1_ffn_Wq"].T, inputs["L1_ffn_Wk"].T,
                               A_.reshape(128, D).T], axis=1))
    BmT = cc(Bm.transpose(0, 2, 1).reshape(128, D))
    b_sb = cc(inputs["L1_ffn_b"].reshape(32, 128).T)
    maps = []
    for c in range(NC):
        maps.append(dict(conC1, attn_tok=attn_tok[c], x_in=x2[c],
                         l1_o_cd=inputs["L1_attn_o_codes"],
                         l1_o_sc=inputs["L1_attn_o_scales"],
                         l1f_WT=cc(np.asarray(inputs["L1_ffn_W"]).T),
                         l1f_b=b_sb, l1f_gates=gates, l1f_BmT=BmT))
    resE = _run(cxE, maps)

    out = np.empty((2, S, D), dtype=np.float32)
    for c in range(NC):
        b, r = core_bt(c)
        out[b, r * TPC:(r + 1) * TPC, :] = resE[c]["x_out"].T
    return out
